# version 11
# baseline (speedup 1.0000x reference)
import numpy as np
import ml_dtypes
import concourse.bass as bass
import concourse.mybir as mybir
from concourse.bass_utils import run_bass_kernel_spmd
from contextlib import ExitStack

F32 = mybir.dt.float32
BF16 = mybir.dt.bfloat16
AF = mybir.ActivationFunctionType
ALU = mybir.AluOpType
AX = mybir.AxisListType

NCORES = 8
T = 4096
D = 1024
TL = 128
NT = T // TL
H = 8
HD = 64
DI = 512
HID = 2730
H2 = 2 * HID
NPAIR = 22
EPS = 1e-5
NEG = -1.0e30

_CONST_CACHE = {}


def _host_consts():
    if _CONST_CACHE:
        return _CONST_CACHE
    il = np.arange(TL)[:, None]
    jl = np.arange(TL)[None, :]
    mCAU = np.where(jl <= il, 0.0, NEG).astype(np.float32)
    mSUP = np.where(jl > il, 0.0, NEG).astype(np.float32)
    mUPI = np.where(jl >= il, 0.0, NEG).astype(np.float32)
    mSUPCAU = np.concatenate([mSUP, mCAU], axis=1)
    mUPICAU = np.concatenate([mUPI, mCAU], axis=1)

    identb = np.eye(TL, dtype=np.float32).astype(ml_dtypes.bfloat16)
    shiftm = np.zeros((TL, TL), np.float32)
    for k in range(TL - 1):
        shiftm[k, k + 1] = 1.0
    pick127 = np.zeros((TL, TL), np.float32)
    pick127[127, 0] = 1.0

    t = np.arange(T, dtype=np.float64)
    half = np.arange(0, HD, 2, dtype=np.float64)

    invf = 1.0 / (10000.0 ** (half / HD))
    fr = t[:, None] * invf[None, :]
    cosf = np.concatenate([np.cos(fr), np.cos(fr)], 1)
    sinf = np.concatenate([np.sin(fr), np.sin(fr)], 1)
    s_o = 1.0 / np.sqrt(HD)
    oc = np.concatenate([np.tile(cosf * s_o, (1, H)), np.tile(cosf, (1, H))], 1)
    os_ = np.concatenate([np.tile(sinf * s_o, (1, H)), np.tile(sinf, (1, H))], 1)

    base = (half + 0.4 * HD) / (1.4 * HD)
    power = (t - T // 2) / 256.0
    scale = base[None, :] ** power[:, None]
    scale = np.concatenate([scale, scale], 1)
    ic = np.concatenate([np.tile(cosf * scale * 8.0, (1, H)),
                         np.tile(cosf / scale, (1, H))], 1)
    is_ = np.concatenate([np.tile(sinf * scale * 8.0, (1, H)),
                          np.tile(sinf / scale, (1, H))], 1)

    perm = []
    for c in range(NPAIR):
        w = 128 if c < 21 else 42
        perm.extend(range(c * 128, c * 128 + w))
        perm.extend(range(HID + c * 128, HID + c * 128 + w))
    perm = np.asarray(perm, dtype=np.int64)
    assert perm.shape[0] == H2

    _CONST_CACHE.update(dict(
        mCAU=mCAU, mSUPCAU=mSUPCAU, mUPICAU=mUPICAU, identb=identb,
        shiftm=shiftm, pick127=pick127,
        oc=oc.astype(np.float32), os=os_.astype(np.float32),
        ic=ic.astype(np.float32), is_=is_.astype(np.float32), perm=perm))
    return _CONST_CACHE


def _pair_geom(c):
    w = 128 if c < 21 else 42
    return w, c * 256


_PHYS = {'T': 'tensor', 'V': 'vector', 'S': 'scalar', 'SD': 'scalar',
         'G': 'gpsimd', 'GD': 'gpsimd', 'Y': 'sync'}
_STEP = {'T': 1, 'V': 1, 'S': 1, 'G': 1, 'SD': 16, 'GD': 16, 'Y': 16}
_ENGS = ('tensor', 'vector', 'scalar', 'gpsimd', 'sync')


class Buf:
    __slots__ = ('w', 'r')

    def __init__(self):
        self.w = None
        self.r = []


class Sched:
    def __init__(self):
        self.thunks = {p: [] for p in _ENGS}
        self.count = dict.fromkeys(_STEP, 0)
        self.waited = {}
        self.barrier = []

    def pass_start(self):
        self.barrier = [(q, c) for q, c in self.count.items() if c > 0]

    def op(self, q, fn, deps=(), reads=(), writes=(), fence_reads=()):
        phys = _PHYS[q]
        dd = list(deps)
        if q == 'GD' and self.count['GD'] > 0:
            dd.append(('GD!', self.count['GD']))
        for b in reads:
            dd.append(b.w)
        for b in writes:
            dd.append(b.w)
            dd.extend(b.r)
        fd = []
        for b in fence_reads:
            dd.append(b.w)
            if b.w is not None and _PHYS[b.w[0]] == phys:
                fd.append(b.w)
        if _STEP[q] == 1:
            for b in reads:
                if (b.w is not None and b.w[0] == q
                        and self.count[q] - b.w[1] <= 2):
                    fd.append(b.w)
        waits = []
        for d in fd:
            dq, val = d
            key = (phys, dq, 'f')
            if self.waited.get(key, -1) >= val:
                continue
            self.waited[key] = val
            waits.append((dq, val))
        for d in dd:
            if d is None:
                continue
            dq, val = d
            if dq == 'GD!':
                key = (phys, 'GD')
                if self.waited.get(key, -1) >= val:
                    continue
                self.waited[key] = val
                waits.append(('GD', val))
                continue
            if _PHYS[dq] == phys and (_STEP[dq] == 1 or dq == q):
                continue
            key = (phys, dq)
            if self.waited.get(key, -1) >= val:
                continue
            self.waited[key] = val
            waits.append((dq, val))
        for b in fence_reads:
            pass
        self.count[q] += _STEP[q]
        self.thunks[phys].append((waits, fn, q))
        dep = (q, self.count[q])
        for b in reads:
            b.r.append(dep)
        for b in fence_reads:
            b.r.append(dep)
        for b in writes:
            b.w = dep
            b.r = []
        return dep


def build(debug=False, passes=('A', 0, 1, 'C'), tapkind='A', eo_scratch=True, tapsel='HQROP'):
    nc = bass.Bass("TRN2", target_bir_lowering=False,
                   dynamic_dma_scratch_size=16384)
    S = Sched()

    x0_d = nc.dram_tensor("x0", [T, D], F32, kind="ExternalInput")
    wqkvO_d = nc.dram_tensor("WqkvO", [D, 3 * DI], F32, kind="ExternalInput")
    woutO_d = nc.dram_tensor("WoutO", [DI, D], F32, kind="ExternalInput")
    wqkvI_d = [nc.dram_tensor(f"wqkv{l}", [D, 3 * DI], F32, kind="ExternalInput") for l in range(2)]
    woI_d = [nc.dram_tensor(f"wo{l}", [DI, D], F32, kind="ExternalInput") for l in range(2)]
    _PW = [1536, 1280, 1280, 1364]
    w1p_d = [[nc.dram_tensor(f"w1q{l}p{i}", [D, _PW[i]], F32, kind="ExternalInput")
              for i in range(4)] for l in range(2)]
    b1p_d = [nc.dram_tensor(f"b1p{l}", [128, 44], F32, kind="ExternalInput") for l in range(2)]
    w2x_d = [nc.dram_tensor(f"w2x{l}", [NPAIR, 128, D], F32, kind="ExternalInput") for l in range(2)]
    qs_d = [nc.dram_tensor(f"qsT{l}", [1, DI], F32, kind="ExternalInput") for l in range(2)]
    ks_d = [nc.dram_tensor(f"ksT{l}", [1, DI], F32, kind="ExternalInput") for l in range(2)]
    lng_d = [nc.dram_tensor(f"lng{l}", [1, D], F32, kind="ExternalInput") for l in range(2)]
    lnb_d = [nc.dram_tensor(f"lnb{l}", [1, D], F32, kind="ExternalInput") for l in range(2)]
    mCAU_d = nc.dram_tensor("mCAU", [TL, TL], F32, kind="ExternalInput")
    mSC_d = nc.dram_tensor("mSUPCAU", [TL, 2 * TL], F32, kind="ExternalInput")
    mUC_d = nc.dram_tensor("mUPICAU", [TL, 2 * TL], F32, kind="ExternalInput")
    idb_d = nc.dram_tensor("identb", [TL, TL], BF16, kind="ExternalInput")
    shm_d = nc.dram_tensor("shiftm", [TL, TL], F32, kind="ExternalInput")
    pk_d = nc.dram_tensor("pick127", [TL, TL], F32, kind="ExternalInput")
    cv_d = nc.dram_tensor("cvec", [128, 6], F32, kind="ExternalInput")
    oc_d = nc.dram_tensor("oc", [T, D], F32, kind="ExternalInput")
    os_d = nc.dram_tensor("osn", [T, D], F32, kind="ExternalInput")
    ic_d = nc.dram_tensor("ic", [T, D], F32, kind="ExternalInput")
    is_d = nc.dram_tensor("isn", [T, D], F32, kind="ExternalInput")

    out_d = nc.dram_tensor("out", [T, D], F32, kind="ExternalOutput")
    dbg = "ExternalOutput" if (debug or eo_scratch) else "Internal"
    x1_d = nc.dram_tensor("x1sc", [T, D], F32, kind=dbg)
    z_d = nc.dram_tensor("zsc", [T, D], F32, kind=dbg)
    za_d = nc.dram_tensor("zasc", [T, D], F32, kind=dbg)
    zsh_d = nc.dram_tensor("zshsc", [T + TL, D], BF16, kind="Internal")
    zash_d = nc.dram_tensor("zashsc", [T + TL, DI], BF16, kind="Internal")
    z2_d = nc.dram_tensor("z2sc", [T, D], F32, kind=dbg)
    if debug:
        tapH_d = nc.dram_tensor("tapH", [T, D], BF16, kind="ExternalOutput")
        tapZ_d = nc.dram_tensor("tapZ", [T, D], F32, kind="ExternalOutput")
        tapRot_d = nc.dram_tensor("tapRot", [T, D], F32, kind="ExternalOutput")
        tapP_d = nc.dram_tensor("tapP", [T, 640], BF16, kind="ExternalOutput")
        tapPT_d = nc.dram_tensor("tapPT", [T, 640], BF16, kind="ExternalOutput")
        tapSM_d = nc.dram_tensor("tapSM", [T, 2], F32, kind="ExternalOutput")
        tapV_d = nc.dram_tensor("tapV", [T, DI], BF16, kind="ExternalOutput")
        tapW_d = nc.dram_tensor("tapW", [128, 8 * 3 * DI], BF16, kind="ExternalOutput")
        tapL2_d = nc.dram_tensor("tapL2", [T, 16], F32, kind="ExternalOutput")
        tapL3_d = nc.dram_tensor("tapL3", [T, 16], F32, kind="ExternalOutput")
        tapSQ_d = nc.dram_tensor("tapSQ", [T, D], F32, kind="ExternalOutput")
        tapQR_d = nc.dram_tensor("tapQR", [T, D], F32, kind="ExternalOutput")
        tapLN2_d = nc.dram_tensor("tapLN2", [128, NT, 2], F32, kind="ExternalOutput")
        tapH2_d = nc.dram_tensor("tapH2", [T, D], BF16, kind="ExternalOutput")
        tapH2F_d = nc.dram_tensor("tapH2F", [T // 2, 8, 256], BF16, kind="ExternalOutput")
        tapPA_d = nc.dram_tensor("tapPA", [T, 512], F32, kind="ExternalOutput")
        tapU_d = nc.dram_tensor("tapU", [T, 256], BF16, kind="ExternalOutput")
        tapT2_d = nc.dram_tensor("tapT2", [T, D], F32, kind="ExternalOutput")
        tapT3_d = nc.dram_tensor("tapT3", [T, D], F32, kind="ExternalOutput")
        tapS_d = nc.dram_tensor("tapS", [T, 8], F32, kind="ExternalOutput")
        tapQK_d = nc.dram_tensor("tapQK", [T, 2 * DI], F32, kind="ExternalOutput")
        tapR_d = nc.dram_tensor("tapR", [T, 2 * DI], BF16, kind="ExternalOutput")
        tapO_d = nc.dram_tensor("tapO", [T, DI], BF16, kind="ExternalOutput")
        tapF_d = nc.dram_tensor("tapF", [T, D], BF16, kind="ExternalOutput")

    top = ExitStack()

    def sbuf(name, shape, dt=F32, es=None):
        return (es or top).enter_context(nc.sbuf_tensor(name, shape, dt))

    def psum(name, shape, dt=F32, es=None):
        return (es or top).enter_context(nc.psum_tensor(name, shape, dt))

    mCAU_s = sbuf("mCAUs", [TL, TL])
    mSC_s = sbuf("mSC", [TL, 2 * TL])
    mUC_s = sbuf("mUC", [TL, 2 * TL])
    idb_s = sbuf("idbs", [TL, TL], BF16)
    shm_s = sbuf("shms", [TL, TL])
    pk_s = sbuf("pks", [TL, TL])
    qsb_s = sbuf("qsb", [128, 2 * DI])
    lngb_s = sbuf("lngb", [128, D])
    lnbb_s = sbuf("lnbb", [128, D])
    ln2_s = sbuf("ln2", [128, NT, 2])
    sq_s = sbuf("sqscr", [128, D])
    b1_s = sbuf("b1s", [128, 44])
    cv_s = sbuf("cvs", [128, 6])
    zrow_s = sbuf("zrow", [1, D], BF16)
    stLN = sbuf("stLN", [128, 8])
    stSM = sbuf("stSM", [128, 2])
    stL2 = sbuf("stL2", [128, 16])
    stL2b = sbuf("stL2b", [128, 16])
    stSH = sbuf("stSH", [128, 6])
    stSHp = sbuf("stSHp", [128, 2])

    bufs = {}

    def B(name):
        if name not in bufs:
            b = Buf()
            b.r = list(S.barrier)
            bufs[name] = b
        return bufs[name]

    bLN, bSM, bL2, bSH, bSHp = B('stLN'), B('stSM'), B('stL2'), B('stSH'), B('stSHp')
    bSQ, bLn2, bB1 = B('sqscr'), B('ln2'), B('b1')

    def ld(q, dst_ap, src_ap, wbuf, deps=()):
        return S.op(q, lambda e, d_=dst_ap, s_=src_ap: e.dma_start(out=d_, in_=s_),
                    deps=deps, writes=(wbuf,))

    ld('Y', cv_s[:], cv_d.ap(), B('cv'))
    S.op('V', lambda e: e.memset(sq_s[0:1, 0:DI], 0.0), writes=(B('sqz'),))
    S.op('V', lambda e: e.memset(zrow_s[:], 0.0), writes=(B('zrow'),))
    S.op('SD', lambda e: e.dma_start(out=zsh_d.ap()[0:1, DI:D], in_=zrow_s[0:1, 0:DI]),
         reads=(B('zrow'),), writes=(B('zsh@-1'),))
    S.op('SD', lambda e: e.dma_start(out=zash_d.ap()[0:1, :], in_=zrow_s[0:1, 0:DI]),
         reads=(B('zrow'),), writes=(B('zash@-1'),))
    ld('Y', mCAU_s[:], mCAU_d.ap(), B('mCAU'))
    ld('Y', mSC_s[:], mSC_d.ap(), B('mSC'))
    ld('Y', mUC_s[:], mUC_d.ap(), B('mUC'))
    ld('Y', idb_s[:], idb_d.ap(), B('idb'))
    ld('Y', shm_s[:], shm_d.ap(), B('shm'))
    ld('Y', pk_s[:], pk_d.ap(), B('pk'))

    def ln_stats(srcfn, src_bufs):
        S.op('V', lambda e: srcfn(e, 'sum'), reads=src_bufs, writes=(bLN,))
        S.op('S', lambda e: srcfn(e, 'sq'), reads=src_bufs, writes=(bSQ, bLN))
        S.op('V', lambda e: e.tensor_scalar(
            out=stLN[:, 2:3], in0=stLN[:, 0:1], scalar1=stLN[:, 0:1],
            scalar2=1.0 / D, op0=ALU.mult, op1=ALU.mult),
            fence_reads=(bLN,), writes=(bLN,))
        S.op('V', lambda e: e.tensor_scalar(
            out=stLN[:, 3:4], in0=stLN[:, 1:2], scalar1=stLN[:, 2:3],
            scalar2=1.0 / D, op0=ALU.subtract, op1=ALU.mult),
            fence_reads=(bLN,), writes=(bLN,))
        S.op('S', lambda e: e.activation(stLN[:, 3:4], stLN[:, 3:4], AF.Ln, bias=cv_s[:, 0:1]),
             reads=(bLN, B('cv')), writes=(bLN,))
        S.op('S', lambda e: e.activation(stLN[:, 4:5], stLN[:, 3:4], AF.Exp, scale=cv_s[:, 1:2]),
             reads=(bLN, B('cv')), writes=(bLN,))
        S.op('V', lambda e: e.tensor_scalar(
            out=stLN[:, 5:6], in0=stLN[:, 0:1], scalar1=stLN[:, 4:5],
            scalar2=-1.0 / D, op0=ALU.mult, op1=ALU.mult),
            fence_reads=(bLN,), writes=(bLN,))

    def attn_pass(kind, l=None):
        S.pass_start()
        es = ExitStack()
        inner = kind == 'B1'
        WIN = 4 if inner else 2
        tag = f"{kind}{l}"
        wqkv_s = sbuf(f"wqkv{tag}", [128, 8, 3 * DI], BF16, es)
        wo_s = sbuf(f"wo{tag}", [128, 4, D], BF16, es)
        kfm = sbuf(f"kfm{tag}", [128, 4, T], BF16, es)
        vst = sbuf(f"vst{tag}", [128, NT, H, HD], BF16, es)
        ztile = [sbuf(f"zt{tag}{i}", [128, D], F32, es) for i in range(3)]
        hbuf = [sbuf(f"hb{tag}{i}", [128, D], BF16, es) for i in range(3)]
        ctab = [sbuf(f"ct{tag}{i}", [128, D], F32, es) for i in range(3)]
        stab = [sbuf(f"sn{tag}{i}", [128, D], F32, es) for i in range(3)]
        hfm = sbuf(f"hfm{tag}", [128, 8, TL], BF16, es)
        qk = sbuf(f"qk{tag}", [128, D], F32, es)
        rot = sbuf(f"rot{tag}", [128, D], F32, es)
        qkr = sbuf(f"qkr{tag}", [128, D], BF16, es)
        qfm = [sbuf(f"qfm{tag}{i}", [128, 4, TL], BF16, es) for i in range(2)]
        Pb = sbuf(f"P{tag}", [128, (WIN + 1) * TL], F32, es)
        Pn = sbuf(f"Pn{tag}", [128, (WIN + 1) * TL], BF16, es)
        PTs = [sbuf(f"PT{tag}{i}", [128, (WIN + 1) * TL], BF16, es) for i in range(2)]
        ofm = [sbuf(f"ofm{tag}{i}", [128, 4, TL], BF16, es) for i in range(2)]
        zares = [sbuf(f"za{tag}{i}", [128, D], F32, es) for i in range(2)]
        lnt = sbuf(f"lnt{tag}", [128, D], F32, es)
        lnb = sbuf(f"lnbf{tag}", [128, D], BF16, es)
        sqP = sbuf(f"sqP{tag}", [128, D], F32, es)
        sqE = sbuf(f"sqE{tag}", [128, D], F32, es)
        stP = sbuf(f"stP{tag}", [128, 8], F32, es)
        stE = sbuf(f"stE{tag}", [128, 8], F32, es)
        stZ = sbuf(f"stZ{tag}", [128, 8], F32, es)
        stL2a = sbuf(f"stL2a{tag}", [128, 16], F32, es)
        stL2b = sbuf(f"stL2b{tag}", [128, 16], F32, es)
        pP1 = psum(f"pP1{tag}", [128, 1024], F32, es)
        pPtr = psum(f"pPtr{tag}", [128, 1024], BF16, es)
        pA0 = psum(f"pA0{tag}", [128, 1024], F32, es)
        pA1 = psum(f"pA1{tag}", [128, 1024], F32, es)
        pO = psum(f"pO{tag}", [128, 128], F32, es)
        pPT0 = pPtr
        pA = [pA0, pA1]
        bpP1 = B(f"pP1{tag}")
        bpPtr = B(f"pPtr{tag}")
        bpA = [B(f"pA0{tag}"), B(f"pA1{tag}")]
        bpPT0, bpO = bpPtr, B(f"pO{tag}")
        bkfm, bvst = B(f"kfm{tag}"), B(f"vst{tag}")
        bwqkv, bwo = B(f"wqkv{tag}"), B(f"wo{tag}")
        bhfm, bqk = B(f"hfm{tag}"), B(f"qk{tag}")
        brot = B(f"rot{tag}")
        bqkr = B(f"qkr{tag}")
        bqfm = [B(f"qfm{tag}0"), B(f"qfm{tag}1")]
        bP, bPn = B(f"P{tag}"), B(f"Pn{tag}")
        bPT = [B(f"PT{tag}0"), B(f"PT{tag}1")]
        bofm = [B(f"ofm{tag}0"), B(f"ofm{tag}1")]
        blnt, blnb = B(f"lnt{tag}"), B(f"lnbf{tag}")
        bsqP, bsqE = B(f"sqP{tag}"), B(f"sqE{tag}")
        bstP, bstE, bstZ = B(f"stP{tag}"), B(f"stE{tag}"), B(f"stZ{tag}")
        bL2a, bL2b = B(f"stL2a{tag}"), B(f"stL2b{tag}")
        bzt = [B(f"zt{tag}{i}") for i in range(3)]
        bhb = [B(f"hb{tag}{i}") for i in range(3)]
        bct = [B(f"ct{tag}{i}") for i in range(3)]
        bsn = [B(f"sn{tag}{i}") for i in range(3)]
        bza = [B(f"za{tag}{i}") for i in range(2)]

        src_d = x0_d if kind == 'A' else z_d
        ct_d = oc_d if kind == 'A' else ic_d
        st_d = os_d if kind == 'A' else is_d

        wq_src = wqkvO_d if kind == 'A' else wqkvI_d[l]
        wo_src = woutO_d if kind == 'A' else woI_d[l]
        ld('GD', wqkv_s[:], wq_src.ap().rearrange("(k p) n -> p k n", p=128), bwqkv)
        ld('GD', wo_s[:], wo_src.ap().rearrange("(k p) n -> p k n", p=128), bwo)
        if inner:
            ld('Y', qsb_s[:, 0:DI], qs_d[l].ap().partition_broadcast(128), B('qsb'))
            ld('Y', qsb_s[:, DI:2 * DI], ks_d[l].ap().partition_broadcast(128), B('qsb'))

        def ln_chain(st, bst, sumfn, sqfn, sb, bsb, extradep=()):
            S.op('V', sumfn, deps=list(extradep), writes=(bst,))
            S.op('S', sqfn, deps=list(extradep), writes=(bsb, bst))
            S.op('V', lambda e: e.tensor_scalar(
                out=st[:, 2:3], in0=st[:, 0:1], scalar1=st[:, 0:1],
                scalar2=1.0 / (D * D), op0=ALU.mult, op1=ALU.mult),
                fence_reads=(bst,), writes=(bst,))
            S.op('V', lambda e: e.tensor_scalar(
                out=st[:, 3:4], in0=st[:, 1:2], scalar1=st[:, 2:3],
                scalar2=1.0, op0=ALU.subtract, op1=ALU.mult),
                fence_reads=(bst,), writes=(bst,))
            S.op('S', lambda e: e.activation(st[:, 3:4], st[:, 3:4], AF.Ln,
                                             bias=cv_s[:, 0:1]),
                 reads=(bst, B('cv')), writes=(bst,))
            S.op('S', lambda e: e.activation(st[:, 4:5], st[:, 3:4], AF.Exp,
                                             scale=cv_s[:, 1:2]),
                 fence_reads=(bst,), writes=(bst,))
            S.op('V', lambda e: e.tensor_scalar(
                out=st[:, 5:6], in0=st[:, 0:1], scalar1=st[:, 4:5],
                scalar2=-1.0 / D, op0=ALU.mult, op1=ALU.mult),
                fence_reads=(bst,), writes=(bst,))

        def tload(t):
            i4 = t % 3
            S.op('Y', lambda e, t_=t: e.dma_start(
                out=ztile[i4][:], in_=src_d.ap()[t_ * TL:(t_ + 1) * TL, :]),
                deps=[B(f"{sname}@{t}").w], writes=(bzt[i4],))
            ld('Y', ctab[i4][:], ct_d.ap()[t * TL:(t + 1) * TL, :], bct[i4])
            ld('Y', stab[i4][:], st_d.ap()[t * TL:(t + 1) * TL, :], bsn[i4])
            if inner:
                S.op('Y', lambda e, t_=t, hb=hbuf[i4]: e.dma_start(
                    out=hb[:], in_=zsh_d.ap()[t_ * TL:(t_ + 1) * TL, :]),
                    deps=[B(f"zsh@{t - 1}").w, B(f"zsh@{t}").w], writes=(bhb[i4],))

        sname = 'x0' if kind == 'A' else 'zs'

        def stage_p1(t):
            if kind != 'A':
                return
            zt, bz = ztile[t % 3], bzt[t % 3]
            h, bh = hbuf[t % 3], bhb[t % 3]

            def sumfn(e, z_=zt):
                return e.reduce_sum(stP[:, 0:1], z_[:], axis=AX.X)

            def sqfn(e, z_=zt):
                return e.activation(sqP[:], z_[:], AF.Square,
                                    scale=cv_s[:, 3:4], accum_out=stP[:, 1:2])
            ln_chain(stP, bstP, sumfn, sqfn, sqP, bsqP, extradep=[bz.w])
            S.op('V', lambda e, z_=zt, h_=h: e.tensor_scalar(
                out=h_[:], in0=z_[:], scalar1=stP[:, 4:5], scalar2=stP[:, 5:6],
                op0=ALU.mult, op1=ALU.add), reads=(bz,), fence_reads=(bstP,),
                writes=(bh,))

        def stage_p(t):
            zt, bz = ztile[t % 3], bzt[t % 3]
            cb, bcb = ctab[t % 3], bct[t % 3]
            sn, bsb = stab[t % 3], bsn[t % 3]
            h, bh = hbuf[t % 3], bhb[t % 3]

            if debug and kind == tapkind and 'H' in tapsel:
                S.op('SD', lambda e, t_=t, h_=h: e.dma_start(
                    out=tapH_d.ap()[t_ * TL:(t_ + 1) * TL, :], in_=h_[:]), reads=(bh,))
            def tr_h(e, h_=h):
                for c in range(8):
                    ins = e.transpose(pPtr[:, c * TL:(c + 1) * TL],
                                      h_[:, c * TL:(c + 1) * TL], idb_s[:])
                return ins
            S.op('T', tr_h, deps=[B('idb').w], reads=(bh,), writes=(bpPtr,))
            S.op('V', lambda e: e.tensor_copy(
                hfm[:].rearrange("p a b -> p (a b)"), pPtr[:]),
                reads=(bpPtr,), writes=(bhfm,))

            def mm_qkv(e, c0, c1, dst, o0, o1):
                for k in range(8):
                    ins = e.matmul(dst[:, o0:o1], hfm[:, k, :],
                                   wqkv_s[:, k, c0:c1], start=(k == 0), stop=(k == 7))
                return ins
            S.op('T', lambda e: mm_qkv(e, 0, DI, pP1, 0, DI),
                 reads=(bhfm, bwqkv), writes=(bpP1,))
            S.op('T', lambda e: mm_qkv(e, DI, 2 * DI, pP1, DI, 2 * DI),
                 reads=(bhfm, bwqkv), writes=(bpP1,))
            S.op('V', lambda e: e.tensor_copy(qk[:], pP1[:]),
                 reads=(bpP1,), writes=(bqk,))
            S.op('T', lambda e: mm_qkv(e, 2 * DI, 3 * DI, pP1, 0, DI),
                 reads=(bhfm, bwqkv), writes=(bpP1,))
            S.op('S', lambda e, t_=t: e.copy(
                vst[:, t_, :, :].rearrange("p a b -> p (a b)"), pP1[:, 0:DI]),
                reads=(bpP1,), writes=(bvst,))

            if inner:
                S.op('V', lambda e: e.tensor_mul(sqP[:], qk[:], qk[:]),
                     reads=(bqk,), writes=(bsqP,))
                S.op('V', lambda e: e.reduce_sum(
                    stL2a[:], sqP[:].rearrange("p (s c) -> p s c", s=16), axis=AX.X),
                    fence_reads=(bsqP,), writes=(bL2a,))
                S.op('V', lambda e: e.tensor_scalar(
                    out=stL2b[:], in0=stL2a[:], scalar1=1e-24, scalar2=None,
                    op0=ALU.max), fence_reads=(bL2a,), writes=(bL2b,))
                S.op('S', lambda e: e.activation(stL2a[:], stL2b[:], AF.Ln),
                     reads=(bL2b,), writes=(bL2a,))
                S.op('S', lambda e: e.activation(stL2b[:], stL2a[:], AF.Exp,
                                                 scale=cv_s[:, 1:2]),
                     reads=(bL2a,), writes=(bL2b,))

                def apply_rstd(e):
                    for i in range(16):
                        ins = e.tensor_scalar(
                            out=qk[:, i * 64:(i + 1) * 64],
                            in0=qk[:, i * 64:(i + 1) * 64],
                            scalar1=stL2b[:, i:i + 1], scalar2=None, op0=ALU.mult)
                    return ins
                S.op('V', apply_rstd, fence_reads=(bL2b,), writes=(bqk,))
                S.op('V', lambda e: e.tensor_mul(qk[:], qk[:], qsb_s[:]),
                     reads=(B('qsb'), bqk), writes=(bqk,))

            S.op('S', lambda e: e.mul(
                rot[:].rearrange("p (s c) -> p s c", s=16)[:, :, 0:32],
                qk[:].rearrange("p (s c) -> p s c", s=16)[:, :, 32:64], cv_s[:, 2:3]),
                reads=(bqk, B('cv')), writes=(brot,))
            S.op('S', lambda e: e.copy(
                rot[:].rearrange("p (s c) -> p s c", s=16)[:, :, 32:64],
                qk[:].rearrange("p (s c) -> p s c", s=16)[:, :, 0:32]),
                reads=(bqk,), writes=(brot,))
            S.op('V', lambda e, c_=cb: e.tensor_mul(qk[:], qk[:], c_[:]),
                 reads=(bcb,), writes=(bqk,))
            S.op('G', lambda e, s_=sn: e.tensor_mul(rot[:], rot[:], s_[:]),
                 reads=(bsb,), writes=(brot,))
            S.op('V', lambda e: e.tensor_add(qkr[:], qk[:], rot[:]),
                 reads=(bqk, brot), writes=(bqkr,))

            if debug and kind == tapkind and 'Q' in tapsel:
                S.op('SD', lambda e, t_=t: e.dma_start(
                    out=tapQK_d.ap()[t_ * TL:(t_ + 1) * TL, :], in_=qk[:]), reads=(bqk,))
            if debug and kind == tapkind and 'R' in tapsel:
                S.op('SD', lambda e, t_=t: e.dma_start(
                    out=tapR_d.ap()[t_ * TL:(t_ + 1) * TL, :], in_=qkr[:]), reads=(bqkr,))
            def tr_qk(e):
                for c in range(8):
                    ins = e.transpose(pPtr[:, c * TL:(c + 1) * TL],
                                      qkr[:, c * TL:(c + 1) * TL], idb_s[:])
                return ins
            S.op('T', tr_qk, deps=[B('idb').w], reads=(bqkr,), writes=(bpPtr,))
            qf = qfm[t % 2]
            S.op('V', lambda e, q_=qf: e.tensor_copy(
                q_[:].rearrange("p a b -> p (a b)"), pPtr[:, 0:DI]),
                reads=(bpPtr,), writes=(bqfm[t % 2],))
            S.op('V', lambda e, t_=t: e.tensor_copy(
                kfm[:, :, t_ * TL:(t_ + 1) * TL],
                pPtr[:, DI:2 * DI].rearrange("p (a b) -> p a b", a=4)),
                reads=(bpPtr,), writes=(bkfm,))

        def stage_a(t):
            t0 = max(0, t - WIN)
            nck = t - t0 + 1
            nk = nck * TL
            own = (t - t0) * TL
            qf, bqf = qfm[t % 2], bqfm[t % 2]
            of, bof = ofm[t % 2], bofm[t % 2]

            def mm_scores(e, hh):
                p_, s_ = hh // 2, hh % 2
                for o in range(0, nk, 512):
                    o2 = min(nk, o + 512)
                    ins = e.matmul(pA[hh % 2][:, o:o2],
                                   qf[s_ * 64:(s_ + 1) * 64, p_, :],
                                   kfm[s_ * 64:(s_ + 1) * 64, p_,
                                       t0 * TL + o:t0 * TL + o2],
                                   start=True, stop=True)
                return ins
            S.op('T', lambda e: mm_scores(e, 0), reads=(bqf, bkfm), writes=(bpA[0],))
            S.op('T', lambda e: mm_scores(e, 1), reads=(bqf, bkfm), writes=(bpA[1],))
            for hh in range(H):
                pg = hh % 2
                if t >= WIN:
                    medge = mSC_s if kind == 'A' else mUC_s
                    S.op('V', lambda e, g=pg, m=medge: e.tensor_tensor(
                        out=bass.AP(pA[g], 0, [[1024, 128], [own, 2], [1, TL]]),
                        in0=bass.AP(pA[g], 0, [[1024, 128], [own, 2], [1, TL]]),
                        in1=m[:].rearrange("p (c f) -> p c f", c=2), op=ALU.add),
                        deps=[B('mSC').w, B('mUC').w],
                        reads=(bpA[pg],), writes=(bpA[pg],))
                else:
                    S.op('V', lambda e, g=pg: e.tensor_tensor(
                        out=pA[g][:, own:own + TL], in0=pA[g][:, own:own + TL],
                        in1=mCAU_s[:], op=ALU.add),
                        deps=[B('mCAU').w], reads=(bpA[pg],), writes=(bpA[pg],))
                S.op('V', lambda e, g=pg: e.reduce_max(
                    stSM[:, 0:1], pA[g][:, 0:nk], axis=AX.X, negate=True),
                    reads=(bpA[pg],), writes=(bSM,))
                S.op('S', lambda e, g=pg: e.activation(
                    Pb[:, 0:nk], pA[g][:, 0:nk], AF.Exp, bias=stSM[:, 0:1],
                    accum_out=stSM[:, 1:2]),
                    reads=(bpA[pg], bSM), writes=(bP, bSM))
                if hh + 2 < H:
                    S.op('T', lambda e, n=hh + 2: mm_scores(e, n),
                         reads=(bqf, bkfm), writes=(bpA[pg],))
                S.op('V', lambda e: e.reciprocal(stSM[:, 0:1], stSM[:, 1:2]),
                     fence_reads=(bSM,), writes=(bSM,))
                S.op('V', lambda e: e.tensor_scalar(
                    out=Pn[:, 0:nk], in0=Pb[:, 0:nk], scalar1=stSM[:, 0:1],
                    scalar2=None, op0=ALU.mult), reads=(bP,), fence_reads=(bSM,),
                    writes=(bPn,))

                if debug and kind == tapkind and hh == 0 and 'P' in tapsel:
                    S.op('SD', lambda e, t_=t, n_=nk: e.dma_start(
                        out=tapP_d.ap()[t_ * TL:(t_ + 1) * TL, 0:n_], in_=Pn[:, 0:n_]),
                        reads=(bPn,))
                def tr_p(e):
                    for c in range(nck):
                        ins = e.transpose(pPT0[:, c * TL:(c + 1) * TL],
                                          Pn[:, c * TL:(c + 1) * TL], idb_s[:])
                    return ins
                S.op('T', tr_p, reads=(bPn,), writes=(bpPT0,))
                S.op('S', lambda e, g=pg: e.copy(PTs[g][:, 0:nk], pPT0[:, 0:nk]),
                     reads=(bpPT0,), writes=(bPT[pg],))

                def mm_av(e, g=pg, h_=hh):
                    for c in range(nck):
                        ins = e.matmul(pO[0:64, :], vst[:, t0 + c, h_, :],
                                       PTs[g][:, c * TL:(c + 1) * TL],
                                       start=(c == 0), stop=(c == nck - 1))
                    return ins
                S.op('T', mm_av, reads=(bPT[pg], bvst), writes=(bpO,))
                S.op('V', lambda e, o_=of, p_=hh // 2, s_=hh % 2: e.tensor_copy(
                    o_[s_ * 64:(s_ + 1) * 64, p_, :], pO[0:64, :]),
                    reads=(bpO,), writes=(bof,))

        def stage_e(t):
            zt, bz = ztile[t % 3], bzt[t % 3]
            of, bof = ofm[t % 2], bofm[t % 2]
            za, bzr = zares[t % 2], bza[t % 2]

            if debug and kind == tapkind and 'O' in tapsel:
                S.op('SD', lambda e, t_=t, o_=of: e.dma_start(
                    out=tapO_d.ap()[t_ * TL:(t_ + 1) * TL, :],
                    in_=o_[:].rearrange("p a b -> p (a b)")), reads=(bof,))
            def mm_wo(e, o0, o1):
                for c in range(4):
                    ins = e.matmul(pA0[:, o0:o1], of[:, c, :], wo_s[:, c, o0:o1],
                                   start=(c == 0), stop=(c == 3))
                return ins
            S.op('T', lambda e: mm_wo(e, 0, DI), reads=(bof, bwo), writes=(bpA[0],))
            S.op('T', lambda e: mm_wo(e, DI, D), reads=(bof, bwo), writes=(bpA[0],))

            if kind == 'A':
                def sumfn(e):
                    return e.reduce_sum(stE[:, 0:1], pA0[:], axis=AX.X)

                def sqfn(e):
                    return e.activation(sqE[:], pA0[:], AF.Square,
                                        scale=cv_s[:, 3:4], accum_out=stE[:, 1:2])
                ln_chain(stE, bstE, sumfn, sqfn, sqE, bsqE, extradep=[bpA[0].w])
                S.op('V', lambda e: e.tensor_scalar(
                    out=lnt[:], in0=pA0[:], scalar1=stE[:, 4:5], scalar2=stE[:, 5:6],
                    op0=ALU.mult, op1=ALU.add), reads=(bpA[0],), fence_reads=(bstE,),
                    writes=(blnt,))
                S.op('V', lambda e, z_=zt: e.tensor_add(za[:], lnt[:], z_[:]),
                     reads=(blnt, bz), writes=(bzr,))
                S.op('SD', lambda e, t_=t: e.dma_start(
                    out=x1_d.ap()[t_ * TL:(t_ + 1) * TL, :], in_=za[:]),
                    reads=(bzr,), writes=(B(f"x1@{t}"),))

                def sumfn2(e):
                    return e.reduce_sum(stZ[:, 0:1], za[:], axis=AX.X)

                def sqfn2(e):
                    return e.activation(sqE[:], za[:], AF.Square,
                                        scale=cv_s[:, 3:4], accum_out=stZ[:, 1:2])
                ln_chain(stZ, bstZ, sumfn2, sqfn2, sqE, bsqE, extradep=[bzr.w])
                S.op('V', lambda e: e.tensor_scalar(
                    out=lnt[:], in0=za[:], scalar1=stZ[:, 4:5], scalar2=stZ[:, 5:6],
                    op0=ALU.mult, op1=ALU.add), reads=(bzr,), fence_reads=(bstZ,),
                    writes=(blnt,))
                S.op('V', lambda e: e.tensor_copy(lnb[:], lnt[:]),
                     reads=(blnt,), writes=(blnb,))
                S.op('SD', lambda e, t_=t: e.dma_start(
                    out=z_d.ap()[t_ * TL:(t_ + 1) * TL, :], in_=lnt[:]),
                    reads=(blnt,), writes=(B(f"zs@{t}"),))
                S.op('SD', lambda e, t_=t: e.dma_start(
                    out=zsh_d.ap()[t_ * TL:(t_ + 1) * TL, 0:DI],
                    in_=lnb[:, 0:DI]), reads=(blnb,), writes=(B(f"zsh@{t}"),))
                S.op('SD', lambda e, t_=t: e.dma_start(
                    out=zsh_d.ap()[t_ * TL + 1:(t_ + 1) * TL + 1, DI:D],
                    in_=lnb[:, DI:D]), reads=(blnb,), writes=(B(f"zsh@{t}"),))
            else:
                S.op('V', lambda e, z_=zt: e.tensor_add(za[:], pA0[:], z_[:]),
                     reads=(bpA[0], bz), writes=(bzr,))
                S.op('SD', lambda e, t_=t: e.dma_start(
                    out=za_d.ap()[t_ * TL:(t_ + 1) * TL, :], in_=za[:]),
                    reads=(bzr,), writes=(B(f"za@{t}"),))
                S.op('V', lambda e: e.tensor_copy(lnb[:, 0:DI], za[:, DI:D]),
                     reads=(bzr,), writes=(blnb,))
                S.op('SD', lambda e, t_=t: e.dma_start(
                    out=zash_d.ap()[t_ * TL + 1:(t_ + 1) * TL + 1, :],
                    in_=lnb[:, 0:DI]), reads=(blnb,), writes=(B(f"zash@{t}"),))
                S.op('V', lambda e: e.reduce_sum(stE[:, 0:1], za[:, 0:DI], axis=AX.X),
                     reads=(bzr,), writes=(bstE,))
                S.op('V', lambda e: e.reduce_sum(stE[:, 2:3], za[:, DI:D], axis=AX.X),
                     reads=(bzr,), writes=(bstE,))
                S.op('S', lambda e: e.activation(sqE[:, 0:DI], za[:, 0:DI], AF.Square,
                                                 accum_out=stE[:, 1:2]),
                     reads=(bzr,), writes=(bsqE, bstE))
                S.op('S', lambda e: e.activation(sqE[:, DI:D], za[:, DI:D], AF.Square,
                                                 accum_out=stE[:, 3:4]),
                     reads=(bzr,), writes=(bsqE, bstE))

                def mm_shift(e, t_=t):
                    ins = e.matmul(pO[:, 0:2], shm_s[:], stE[:, 2:4],
                                   start=True, stop=(t_ == 0))
                    if t_ > 0:
                        ins = e.matmul(pO[:, 0:2], pk_s[:], stSHp[:],
                                       start=False, stop=True)
                    return ins
                S.op('T', mm_shift, deps=[B('shm').w, B('pk').w],
                     reads=(bstE, bSHp), writes=(bpO,))
                S.op('V', lambda e: e.tensor_copy(stSHp[:], stE[:, 2:4]),
                     reads=(bstE,), writes=(bSHp,))
                S.op('V', lambda e: e.tensor_add(stSH[:, 4:6], stE[:, 0:2],
                                                 pO[:, 0:2]),
                     reads=(bstE, bpO), writes=(bSH,))
                S.op('V', lambda e: e.tensor_scalar(
                    out=stSH[:, 0:1], in0=stSH[:, 4:5], scalar1=stSH[:, 4:5],
                    scalar2=1.0 / D, op0=ALU.mult, op1=ALU.mult),
                    fence_reads=(bSH,), writes=(bSH,))
                S.op('V', lambda e: e.tensor_scalar(
                    out=stSH[:, 1:2], in0=stSH[:, 5:6], scalar1=stSH[:, 0:1],
                    scalar2=1.0 / D, op0=ALU.subtract, op1=ALU.mult),
                    fence_reads=(bSH,), writes=(bSH,))
                S.op('S', lambda e: e.activation(stSH[:, 1:2], stSH[:, 1:2], AF.Ln,
                                                 bias=cv_s[:, 0:1]),
                     reads=(bSH, B('cv')), writes=(bSH,))
                S.op('S', lambda e, t_=t: e.activation(
                    ln2_s[:, t_, 0:1], stSH[:, 1:2], AF.Exp, scale=cv_s[:, 1:2]),
                    reads=(bSH,), writes=(bLn2,))
                S.op('V', lambda e, t_=t: e.tensor_scalar(
                    out=ln2_s[:, t_, 1:2], in0=stSH[:, 4:5],
                    scalar1=ln2_s[:, t_, 0:1], scalar2=-1.0 / D,
                    op0=ALU.mult, op1=ALU.mult), reads=(bSH,),
                    fence_reads=(bLn2,), writes=(bLn2,))

        tload(0)
        tload(1)
        tload(2)
        stage_p1(0)
        stage_p1(1)
        stage_p1(2)
        stage_p(0)
        stage_p(1)
        for t in range(NT):
            stage_a(t)
            stage_e(t)
            if t + 3 < NT:
                tload(t + 3)
                stage_p1(t + 3)
            if t + 2 < NT:
                stage_p(t + 2)
        es.close()


    def ffn_pass(l, last):
        S.pass_start()
        es = ExitStack()
        tag = f"F{l}"
        PARTS = [(0, 6), (6, 11), (11, 16), (16, 22)]
        PLO = []
        w1sp = []
        for i, (c0, c1) in enumerate(PARTS):
            lo = _pair_geom(c0)[1]
            wl, ol = _pair_geom(c1 - 1)
            hi = ol + 2 * wl
            PLO.append((lo, hi))
            w1sp.append(sbuf(f"w1s{tag}p{i}", [128, 8, hi - lo], BF16, es))
        w2s = sbuf(f"w2s{tag}", [128, NPAIR, D], BF16, es)
        zag = [sbuf(f"zag{tag}{i}", [128, 2 * D], F32, es) for i in range(3)]
        zsh = [sbuf(f"zsh{tag}{i}", [128, 2 * DI], BF16, es) for i in range(3)]
        h2 = sbuf(f"h2{tag}", [128, 2 * D], BF16, es)
        h2fm = sbuf(f"h2fm{tag}", [128, 8, 256], BF16, es)
        lt = sbuf(f"lt{tag}", [128, D], F32, es)
        gg = sbuf(f"gg{tag}", [128, 256], BF16, es)
        ub = [sbuf(f"ub{tag}{i}", [128, 256], BF16, es) for i in range(2)]
        u21 = sbuf(f"u21{tag}", [43, 256], BF16, es)
        zn = [sbuf(f"zn{tag}0", [128, 2 * D], F32, es)] * 2
        znb16 = None if last else sbuf(f"znb16{tag}", [128, 2 * D], BF16, es)
        pZ0 = psum(f"pZ0{tag}", [128, 1024], F32, es)
        pZ1 = psum(f"pZ1{tag}", [128, 1024], F32, es)
        pA = psum(f"pA{tag}", [128, 512], F32, es)
        pB = psum(f"pB{tag}", [128, 512], F32, es)
        pT2 = psum(f"pT2{tag}", [128, 2048], BF16, es)
        pZ = [pZ0, pZ1]
        pAG = [pA, pB]
        bpZ = [B(f"pZ0{tag}"), B(f"pZ1{tag}")]
        bpAG = [B(f"pA{tag}"), B(f"pB{tag}")]
        bpT2 = B(f"pT2{tag}")
        bzag = [B(f"zag{tag}{i}") for i in range(3)]
        bzsh = [B(f"zsh{tag}{i}") for i in range(3)]
        bh2, bh2fm, blt = B(f"h2{tag}"), B(f"h2fm{tag}"), B(f"lt{tag}")
        bgg = B(f"gg{tag}")
        bub = [B(f"ub{tag}{i}") for i in range(2)]
        bu21 = B(f"u21{tag}")
        bzn = [B(f"zn{tag}0")] * 2
        bw1, bw2 = B(f"w1s{tag}"), B(f"w2s{tag}")

        dst_d = z2_d if last else z_d
        partof = [next(i for i, (c0, c1) in enumerate(PARTS) if c0 <= c < c1)
                  for c in range(NPAIR)]
        w1deps, w2deps = [], []
        for i, (c0, c1) in enumerate(PARTS):
            lo, hi = PLO[i]
            import os as _os5
            if _os5.environ.get('NOW1'):
                d = S.op('V', lambda e, i_=i: e.memset(w1sp[i_][:], 0.0), writes=(bw1,))
            else:
                d = S.op('GD', lambda e, i_=i: e.dma_start(
                    out=w1sp[i_][:],
                    in_=w1p_d[l][i_].ap().rearrange("(k p) n -> p k n", p=128)),
                    writes=(bw1,))
            w1deps.append(d)
            import os as _os4
            if _os4.environ.get('NOW2'):
                d = S.op('V', lambda e, c0_=c0, c1_=c1: e.memset(
                    w2s[:, c0_:c1_, :], 0.0), writes=(bw2,))
            else:
                d = S.op('GD', lambda e, c0_=c0, c1_=c1: e.dma_start(
                    out=w2s[:, c0_:c1_, :],
                    in_=w2x_d[l].ap()[c0_:c1_, :, :].rearrange("c p d -> p c d")),
                    writes=(bw2,))
            w2deps.append(d)
        ld('Y', b1_s[:], b1p_d[l].ap(), bB1)
        ld('Y', lngb_s[:], lng_d[l].ap().partition_broadcast(128), B('lngb'))
        ld('Y', lnbb_s[:], lnb_d[l].ap().partition_broadcast(128), B('lnbb'))
        S.op('V', lambda e: e.memset(u21[0:43, :], 1.0), writes=(bu21,))

        NG = NT // 2

        def gload(g):
            i = g % 3
            dz = [B(f"za@{2 * g}").w, B(f"za@{2 * g + 1}").w]
            S.op('Y', lambda e, g_=g: e.dma_start(
                out=zag[i][:].rearrange("p (a d) -> p a d", a=2),
                in_=za_d.ap()[g_ * 256:(g_ + 1) * 256, :].rearrange(
                    "(a p) d -> p a d", p=128)), deps=dz, writes=(bzag[i],))
            dsh = [B(f"zash@{2 * g - 1}").w, B(f"zash@{2 * g}").w,
                   B(f"zash@{2 * g + 1}").w]
            S.op('Y', lambda e, g_=g: e.dma_start(
                out=zsh[i][:].rearrange("p (a d) -> p a d", a=2),
                in_=zash_d.ap()[g_ * 256:(g_ + 1) * 256, :].rearrange(
                    "(a p) d -> p a d", p=128)), deps=dsh, writes=(bzsh[i],))

        def emit_group(g):
            zb, sh = zag[g % 3], zsh[g % 3]
            bzb, bsh = bzag[g % 3], bzsh[g % 3]

            for a in range(2):
                tt = 2 * g + a
                S.op('V', lambda e, a_=a, tt_=tt: e.tensor_scalar(
                    out=lt[:, 0:DI], in0=zb[:, a_ * D:a_ * D + DI],
                    scalar1=ln2_s[:, tt_, 0:1], scalar2=ln2_s[:, tt_, 1:2],
                    op0=ALU.mult, op1=ALU.add), reads=(bzb, bLn2), writes=(blt,))
                S.op('V', lambda e, a_=a, tt_=tt: e.tensor_scalar(
                    out=lt[:, DI:D], in0=sh[:, a_ * DI:(a_ + 1) * DI],
                    scalar1=ln2_s[:, tt_, 0:1], scalar2=ln2_s[:, tt_, 1:2],
                    op0=ALU.mult, op1=ALU.add), reads=(bsh, bLn2), writes=(blt,))
                S.op('V', lambda e: e.tensor_mul(lt[:], lt[:], lngb_s[:]),
                     reads=(B('lngb'), blt), writes=(blt,))
                S.op('V', lambda e, a_=a: e.tensor_add(
                    h2[:, a_ * D:(a_ + 1) * D], lt[:], lnbb_s[:]),
                    reads=(B('lnbb'), blt), writes=(bh2,))

            if debug and l == 0:
                if g == 0:
                    S.op('SD', lambda e: e.dma_start(out=tapLN2_d.ap(), in_=ln2_s[:]),
                         deps=[bLn2.w])
                S.op('SD', lambda e, g_=g: e.dma_start(
                    out=tapH2_d.ap()[g_ * 256:(g_ + 1) * 256, :].rearrange(
                        "(a p) d -> p a d", p=128),
                    in_=h2[:].rearrange("p (a d) -> p a d", a=2)), reads=(bh2,))
            def tr_h2(e):
                for a in range(2):
                    for c in range(8):
                        ins = e.transpose(
                            pT2[:, (a * 8 + c) * TL:(a * 8 + c + 1) * TL],
                            h2[:, a * D + c * TL:a * D + (c + 1) * TL], idb_s[:])
                return ins
            S.op('T', tr_h2, deps=[B('idb').w], reads=(bh2,), writes=(bpT2,))
            S.op('V', lambda e: e.tensor_copy(
                h2fm[:, :, 0:128], pT2[:, 0:1024].rearrange("p (a b) -> p a b", a=8)),
                reads=(bpT2,), writes=(bh2fm,))
            S.op('V', lambda e: e.tensor_copy(
                h2fm[:, :, 128:256],
                pT2[:, 1024:2048].rearrange("p (a b) -> p a b", a=8)),
                reads=(bpT2,), writes=(bh2fm,))

            znb, bznb = zn[g % 2], bzn[g % 2]
            if debug and l == 0:
                S.op('SD', lambda e, g_=g: e.dma_start(
                    out=tapH2F_d.ap()[g_ * 128:(g_ + 1) * 128, :, :], in_=h2fm[:]),
                    reads=(bh2fm,))
            def mm_w1(e, c):
                w, off = _pair_geom(c)
                pag = pAG[c % 2]
                pt = partof[c]
                wsp = w1sp[pt]
                off = off - PLO[pt][0]
                for k in range(8):
                    e.matmul(pag[0:w, 0:256], wsp[:, k, off:off + w],
                             h2fm[:, k, :], start=(k == 0), stop=False)
                for k in range(8):
                    ins = e.matmul(pag[0:w, 256:512],
                                   wsp[:, k, off + w:off + 2 * w],
                                   h2fm[:, k, :], start=(k == 0), stop=(k == 7))
                return ins

            def mm_w2(e, c):
                w, _ = _pair_geom(c)
                wk = w if c < 21 else 43
                u_ = ub[c % 2] if c < 21 else u21
                for a in range(2):
                    pz = pZ[a]
                    for hf in range(2):
                        ins = e.matmul(pz[:, hf * DI:(hf + 1) * DI],
                                       u_[0:wk, a * TL:(a + 1) * TL],
                                       w2s[0:wk, c, hf * DI:(hf + 1) * DI],
                                       start=(c == 0), stop=(c == NPAIR - 1))
                return ins
            S.op('T', lambda e: mm_w1(e, 0), deps=[w1deps[0]],
                 reads=(bh2fm,), writes=(bpAG[0],))
            for c in range(NPAIR):
                w, off = _pair_geom(c)
                pag, bpag = pAG[c % 2], bpAG[c % 2]
                uu = ub[c % 2] if c < 21 else u21
                buu = bub[c % 2] if c < 21 else bu21
                S.op('S', lambda e, w_=w, c_=c, pg=pag: e.activation(
                    gg[0:w_, :], pg[0:w_, 256:512], AF.Gelu,
                    bias=b1_s[0:w_, 2 * c_ + 1:2 * c_ + 2]),
                    reads=(bpag, bB1), writes=(bgg,))
                if c + 1 < NPAIR:
                    S.op('T', lambda e, c_=c + 1: mm_w1(e, c_),
                         deps=[w1deps[partof[c + 1]]],
                         reads=(bh2fm,), writes=(bpAG[(c + 1) % 2],))
                S.op('V', lambda e, w_=w, c_=c, pg=pag, u_=uu: e.scalar_tensor_tensor(
                    out=u_[0:w_, :], in0=pg[0:w_, 0:256],
                    scalar=b1_s[0:w_, 2 * c_:2 * c_ + 1], in1=gg[0:w_, :],
                    op0=ALU.add, op1=ALU.mult),
                    reads=(bpag, bgg, bB1), writes=(buu,))
                if debug and l == 0 and c == 2:
                    S.op('SD', lambda e, g_=g, u_=uu: e.dma_start(
                        out=tapU_d.ap()[g_ * 128:(g_ + 1) * 128, :], in_=u_[0:128, :]),
                        reads=(buu,))
                S.op('T', lambda e, c_=c: mm_w2(e, c_),
                     deps=[w2deps[partof[c]]],
                     reads=(buu,), writes=(bpZ[0], bpZ[1]))

            for a in range(2):
                S.op('V', lambda e, a_=a: e.tensor_add(
                    znb[:, a_ * D:(a_ + 1) * D], pZ[a_][:],
                    zb[:, a_ * D:(a_ + 1) * D]),
                    reads=(bpZ[a], bzb), writes=(bznb,))
            sname2 = 'z2' if last else 'zs'
            S.op('SD', lambda e, g_=g: e.dma_start(
                out=dst_d.ap()[g_ * 256:(g_ + 1) * 256, :].rearrange(
                    "(a p) d -> p a d", p=128),
                in_=znb[:].rearrange("p (a d) -> p a d", a=2)), reads=(bznb,),
                writes=(B(f"{sname2}@{2 * g}"), B(f"{sname2}@{2 * g + 1}")))
            if not last:
                S.op('V', lambda e: e.tensor_copy(znb16[:], znb[:]),
                     reads=(bznb,), writes=(B(f"znb16{tag}"),))
                S.op('SD', lambda e, g_=g: e.dma_start(
                    out=zsh_d.ap()[g_ * 256:(g_ + 1) * 256, 0:DI].rearrange(
                        "(a p) d -> p a d", p=128),
                    in_=znb16[:].rearrange("p (a d) -> p a d", a=2)[:, :, 0:DI]),
                    reads=(B(f"znb16{tag}"),),
                    writes=(B(f"zsh@{2 * g}"), B(f"zsh@{2 * g + 1}")))
                S.op('SD', lambda e, g_=g: e.dma_start(
                    out=zsh_d.ap()[g_ * 256 + 1:(g_ + 1) * 256 + 1, DI:D].rearrange(
                        "(a p) d -> p a d", p=128),
                    in_=znb16[:].rearrange("p (a d) -> p a d", a=2)[:, :, DI:D]),
                    reads=(B(f"znb16{tag}"),),
                    writes=(B(f"zsh@{2 * g}"), B(f"zsh@{2 * g + 1}")))

        gload(0)
        gload(1)
        for g in range(NG):
            if g + 2 < NG:
                gload(g + 2)
            emit_group(g)
        es.close()

    def final_pass():
        S.pass_start()
        es = ExitStack()
        z2t = [sbuf(f"cz{i}", [128, D], F32, es) for i in range(3)]
        x1t = [sbuf(f"cx{i}", [128, D], F32, es) for i in range(3)]
        ot = [sbuf(f"co{i}", [128, D], F32, es) for i in range(2)]
        lt = sbuf("clt", [128, D], F32, es)
        bcz = [B(f"cz{i}") for i in range(3)]
        bcx = [B(f"cx{i}") for i in range(3)]
        bco = [B("co0"), B("co1")]
        bclt = B("clt")

        def cload(t):
            S.op('Y', lambda e, t_=t: e.dma_start(
                out=z2t[t_ % 3][:], in_=z2_d.ap()[t_ * TL:(t_ + 1) * TL, :]),
                deps=[B(f"z2@{t}").w], writes=(bcz[t % 3],))
            S.op('Y', lambda e, t_=t: e.dma_start(
                out=x1t[t_ % 3][:], in_=x1_d.ap()[t_ * TL:(t_ + 1) * TL, :]),
                deps=[B(f"x1@{t}").w], writes=(bcx[t % 3],))
        cload(0)
        cload(1)
        for t in range(NT):
            if t + 2 < NT:
                cload(t + 2)
            zt, bz = z2t[t % 3], bcz[t % 3]

            def lnsrc(e, what, z_=zt):
                if what == 'sum':
                    return e.reduce_sum(stLN[:, 0:1], z_[:], axis=AX.X)
                return e.activation(sq_s[:], z_[:], AF.Square,
                                    accum_out=stLN[:, 1:2])
            ln_stats(lnsrc, (bz,))
            S.op('V', lambda e, z_=zt: e.tensor_scalar(
                out=lt[:], in0=z_[:], scalar1=stLN[:, 4:5], scalar2=stLN[:, 5:6],
                op0=ALU.mult, op1=ALU.add), reads=(bz,), fence_reads=(bLN,),
                writes=(bclt,))
            S.op('V', lambda e, x_=x1t[t % 3], o_=ot[t % 2]: e.tensor_add(
                o_[:], lt[:], x_[:]),
                reads=(bclt, bcx[t % 3]), writes=(bco[t % 2],))
            S.op('SD', lambda e, t_=t, o_=ot[t % 2]: e.dma_start(
                out=out_d.ap()[t_ * TL:(t_ + 1) * TL, :], in_=o_[:]),
                reads=(bco[t % 2],))
        es.close()

    if 'A' in passes:
        attn_pass('A')
    for l in range(2):
        if l in passes or f'B1{l}' in passes:
            attn_pass('B1', l)
        if l in passes or f'F{l}' in passes:
            ffn_pass(l, last=(l == 1))
    if 'C' in passes:
        final_pass()

    emit_es = ExitStack()
    sems = {q: emit_es.enter_context(nc.semaphore(f"sem{q}")) for q in _STEP}
    block = emit_es.enter_context(nc.Block())
    with emit_es:
        def emit_engine(phys):
            def body(h):
                for waits, fn, q in S.thunks[phys]:
                    for (dq, val) in waits:
                        h.wait_ge(sems[dq], val)
                    ins = fn(h)
                    ins.then_inc(sems[q], _STEP[q])
                if phys == 'sync':
                    for q2 in ('SD', 'GD', 'Y'):
                        if S.count[q2] > 0:
                            h.wait_ge(sems[q2], S.count[q2])
            return body
        block.tensor(emit_engine('tensor'))
        block.vector(emit_engine('vector'))
        block.scalar(emit_engine('scalar'))
        block.gpsimd(emit_engine('gpsimd'))
        block.sync(emit_engine('sync'))
    top.close()
    return nc


def kernel(L0_attn_x, L0_attn_Wqkv, L0_attn_Wout, L0_ffn_wqkv, L0_ffn_q_scale,
           L0_ffn_k_scale, L0_ffn_wo, L0_ffn_ln_g, L0_ffn_ln_b, L0_ffn_w1,
           L0_ffn_b1, L0_ffn_w2, L0_ffn_b2):
    C = _host_consts()
    perm = C['perm']
    f32 = np.float32

    x = np.ascontiguousarray(np.asarray(L0_attn_x, dtype=f32))
    shared = {
        "WqkvO": np.ascontiguousarray(np.asarray(L0_attn_Wqkv, f32)),
        "WoutO": np.ascontiguousarray(np.asarray(L0_attn_Wout, f32)),
        "mCAU": C['mCAU'], "mSUPCAU": C['mSUPCAU'], "mUPICAU": C['mUPICAU'],
        "identb": C['identb'], "shiftm": C['shiftm'], "pick127": C['pick127'],
        "oc": C['oc'], "osn": C['os'], "ic": C['ic'], "isn": C['is_'],
        "cvec": np.tile(np.array([[EPS, -0.5, -1.0, D ** -0.5, 1.0 / D, 0.0]], np.float32), (128, 1)),
    }
    for l in range(2):
        shared[f"wqkv{l}"] = np.ascontiguousarray(np.asarray(L0_ffn_wqkv[l], f32))
        shared[f"wo{l}"] = np.ascontiguousarray(np.asarray(L0_ffn_wo[l], f32))
        w1p_full = np.asarray(L0_ffn_w1[l], f32)[:, perm]
        _plo = [(0, 1536), (1536, 2816), (2816, 4096), (4096, 5460)]
        for i, (lo, hi) in enumerate(_plo):
            shared[f"w1q{l}p{i}"] = np.ascontiguousarray(w1p_full[:, lo:hi])
        b1p = np.asarray(L0_ffn_b1[l], f32)[perm]
        b1s = np.zeros((128, 44), f32)
        for c in range(NPAIR):
            w, off = _pair_geom(c)
            b1s[0:w, 2 * c] = b1p[off:off + w]
            b1s[0:w, 2 * c + 1] = b1p[off + w:off + 2 * w]
        shared[f"b1p{l}"] = b1s
        w2 = np.asarray(L0_ffn_w2[l], f32)
        w2x = np.zeros((NPAIR, 128, D), f32)
        for c in range(NPAIR):
            w = 128 if c < 21 else 42
            w2x[c, 0:w, :] = w2[c * 128:c * 128 + w, :]
        w2x[21, 42, :] = np.asarray(L0_ffn_b2[l], f32)
        shared[f"w2x{l}"] = np.ascontiguousarray(w2x)
        shared[f"qsT{l}"] = np.ascontiguousarray(
            np.tile(np.asarray(L0_ffn_q_scale[l], f32), H)[None, :])
        shared[f"ksT{l}"] = np.ascontiguousarray(
            np.tile(np.asarray(L0_ffn_k_scale[l], f32), H)[None, :])
        shared[f"lng{l}"] = np.ascontiguousarray(np.asarray(L0_ffn_ln_g[l], f32)[None, :])
        shared[f"lnb{l}"] = np.ascontiguousarray(np.asarray(L0_ffn_ln_b[l], f32)[None, :])

    nc = build()
    in_maps = [dict(shared, x0=np.ascontiguousarray(x[b])) for b in range(NCORES)]
    res = run_bass_kernel_spmd(nc, in_maps, core_ids=list(range(NCORES)))
    out = np.stack([np.asarray(res.results[b]["out"]) for b in range(NCORES)], axis=0)
    return out.astype(np.float32)


# version 15
# speedup vs baseline: 1.0287x; 1.0287x over previous
import numpy as np
import ml_dtypes
import concourse.bass as bass
import concourse.mybir as mybir
from concourse.bass_utils import run_bass_kernel_spmd
from contextlib import ExitStack

F32 = mybir.dt.float32
BF16 = mybir.dt.bfloat16
AF = mybir.ActivationFunctionType
ALU = mybir.AluOpType
AX = mybir.AxisListType

NCORES = 8
T = 4096
D = 1024
TL = 128
NT = T // TL
H = 8
HD = 64
DI = 512
HID = 2730
H2 = 2 * HID
NPAIR = 22
EPS = 1e-5
NEG = -1.0e30

_CONST_CACHE = {}


def _host_consts():
    if _CONST_CACHE:
        return _CONST_CACHE
    il = np.arange(TL)[:, None]
    jl = np.arange(TL)[None, :]
    mCAU = np.where(jl <= il, 0.0, NEG).astype(np.float32)
    mSUP = np.where(jl > il, 0.0, NEG).astype(np.float32)
    mUPI = np.where(jl >= il, 0.0, NEG).astype(np.float32)
    mSUPCAU = np.concatenate([mSUP, mCAU], axis=1)
    mUPICAU = np.concatenate([mUPI, mCAU], axis=1)

    identb = np.eye(TL, dtype=np.float32).astype(ml_dtypes.bfloat16)
    shiftm = np.zeros((TL, TL), np.float32)
    for k in range(TL - 1):
        shiftm[k, k + 1] = 1.0
    pick127 = np.zeros((TL, TL), np.float32)
    pick127[127, 0] = 1.0

    t = np.arange(T, dtype=np.float64)
    half = np.arange(0, HD, 2, dtype=np.float64)

    invf = 1.0 / (10000.0 ** (half / HD))
    fr = t[:, None] * invf[None, :]
    cosf = np.concatenate([np.cos(fr), np.cos(fr)], 1)
    sinf = np.concatenate([np.sin(fr), np.sin(fr)], 1)
    s_o = 1.0 / np.sqrt(HD)
    oc = np.concatenate([np.tile(cosf * s_o, (1, H)), np.tile(cosf, (1, H))], 1)
    os_ = np.concatenate([np.tile(sinf * s_o, (1, H)), np.tile(sinf, (1, H))], 1)

    base = (half + 0.4 * HD) / (1.4 * HD)
    power = (t - T // 2) / 256.0
    scale = base[None, :] ** power[:, None]
    scale = np.concatenate([scale, scale], 1)
    ic = np.concatenate([np.tile(cosf * scale * 8.0, (1, H)),
                         np.tile(cosf / scale, (1, H))], 1)
    is_ = np.concatenate([np.tile(sinf * scale * 8.0, (1, H)),
                          np.tile(sinf / scale, (1, H))], 1)

    perm = []
    for c in range(NPAIR):
        w = 128 if c < 21 else 42
        perm.extend(range(c * 128, c * 128 + w))
        perm.extend(range(HID + c * 128, HID + c * 128 + w))
    perm = np.asarray(perm, dtype=np.int64)
    assert perm.shape[0] == H2

    _CONST_CACHE.update(dict(
        mCAU=mCAU, mSUPCAU=mSUPCAU, mUPICAU=mUPICAU, identb=identb,
        shiftm=shiftm, pick127=pick127,
        oc=oc.astype(np.float32), os=os_.astype(np.float32),
        ic=ic.astype(np.float32), is_=is_.astype(np.float32), perm=perm))
    return _CONST_CACHE


def _pair_geom(c):
    w = 128 if c < 21 else 42
    return w, c * 256


_PHYS = {'T': 'tensor', 'V': 'vector', 'S': 'scalar', 'SD': 'scalar',
         'G': 'gpsimd', 'GD': 'gpsimd', 'Y': 'sync'}
_STEP = {'T': 1, 'V': 1, 'S': 1, 'G': 1, 'SD': 16, 'GD': 16, 'Y': 16}
_ENGS = ('tensor', 'vector', 'scalar', 'gpsimd', 'sync')


class Buf:
    __slots__ = ('w', 'r')

    def __init__(self):
        self.w = None
        self.r = []


class Sched:
    def __init__(self):
        self.thunks = {p: [] for p in _ENGS}
        self.count = dict.fromkeys(_STEP, 0)
        self.waited = {}
        self.barrier = []

    def pass_start(self):
        self.barrier = [(q, c) for q, c in self.count.items() if c > 0]

    def op(self, q, fn, deps=(), reads=(), writes=(), fence_reads=()):
        phys = _PHYS[q]
        dd = list(deps)
        if q == 'GD' and self.count['GD'] > 0:
            dd.append(('GD!', self.count['GD']))
        for b in reads:
            dd.append(b.w)
        for b in writes:
            dd.append(b.w)
            dd.extend(b.r)
        fd = []
        for b in fence_reads:
            dd.append(b.w)
            if b.w is not None and _PHYS[b.w[0]] == phys:
                fd.append(b.w)
        if _STEP[q] == 1:
            for b in reads:
                if (b.w is not None and b.w[0] == q
                        and self.count[q] - b.w[1] <= 2):
                    fd.append(b.w)
        waits = []
        for d in fd:
            dq, val = d
            key = (phys, dq, 'f')
            if self.waited.get(key, -1) >= val:
                continue
            self.waited[key] = val
            waits.append((dq, val))
        for d in dd:
            if d is None:
                continue
            dq, val = d
            if dq == 'GD!':
                key = (phys, 'GD')
                if self.waited.get(key, -1) >= val:
                    continue
                self.waited[key] = val
                waits.append(('GD', val))
                continue
            if _PHYS[dq] == phys and (_STEP[dq] == 1 or dq == q):
                continue
            key = (phys, dq)
            if self.waited.get(key, -1) >= val:
                continue
            self.waited[key] = val
            waits.append((dq, val))
        for b in fence_reads:
            pass
        self.count[q] += _STEP[q]
        self.thunks[phys].append((waits, fn, q))
        dep = (q, self.count[q])
        for b in reads:
            b.r.append(dep)
        for b in fence_reads:
            b.r.append(dep)
        for b in writes:
            b.w = dep
            b.r = []
        return dep


def build(debug=False, passes=('A', 0, 1, 'C'), tapkind='A', eo_scratch=True, tapsel='HQROP'):
    nc = bass.Bass("TRN2", target_bir_lowering=False,
                   dynamic_dma_scratch_size=16384)
    S = Sched()

    x0_d = nc.dram_tensor("x0", [T, D], F32, kind="ExternalInput")
    wqkvO_d = nc.dram_tensor("WqkvO", [D, 3 * DI], F32, kind="ExternalInput")
    woutO_d = nc.dram_tensor("WoutO", [DI, D], F32, kind="ExternalInput")
    wqkvI_d = [nc.dram_tensor(f"wqkv{l}", [D, 3 * DI], F32, kind="ExternalInput") for l in range(2)]
    woI_d = [nc.dram_tensor(f"wo{l}", [DI, D], F32, kind="ExternalInput") for l in range(2)]
    _PW = [1536, 1280, 1280, 1364]
    w1p_d = [[nc.dram_tensor(f"w1q{l}p{i}", [D, _PW[i]], F32, kind="ExternalInput")
              for i in range(4)] for l in range(2)]
    b1p_d = [nc.dram_tensor(f"b1p{l}", [128, 44], F32, kind="ExternalInput") for l in range(2)]
    w2x_d = [nc.dram_tensor(f"w2x{l}", [NPAIR, 128, D], F32, kind="ExternalInput") for l in range(2)]
    qs_d = [nc.dram_tensor(f"qsT{l}", [1, DI], F32, kind="ExternalInput") for l in range(2)]
    ks_d = [nc.dram_tensor(f"ksT{l}", [1, DI], F32, kind="ExternalInput") for l in range(2)]
    lng_d = [nc.dram_tensor(f"lng{l}", [1, D], F32, kind="ExternalInput") for l in range(2)]
    lnb_d = [nc.dram_tensor(f"lnb{l}", [1, D], F32, kind="ExternalInput") for l in range(2)]
    mCAU_d = nc.dram_tensor("mCAU", [TL, TL], F32, kind="ExternalInput")
    mSC_d = nc.dram_tensor("mSUPCAU", [TL, 2 * TL], F32, kind="ExternalInput")
    mUC_d = nc.dram_tensor("mUPICAU", [TL, 2 * TL], F32, kind="ExternalInput")
    idb_d = nc.dram_tensor("identb", [TL, TL], BF16, kind="ExternalInput")
    shm_d = nc.dram_tensor("shiftm", [TL, TL], F32, kind="ExternalInput")
    pk_d = nc.dram_tensor("pick127", [TL, TL], F32, kind="ExternalInput")
    cv_d = nc.dram_tensor("cvec", [128, 6], F32, kind="ExternalInput")
    oc_d = nc.dram_tensor("oc", [T, D], F32, kind="ExternalInput")
    os_d = nc.dram_tensor("osn", [T, D], F32, kind="ExternalInput")
    ic_d = nc.dram_tensor("ic", [T, D], F32, kind="ExternalInput")
    is_d = nc.dram_tensor("isn", [T, D], F32, kind="ExternalInput")

    out_d = nc.dram_tensor("out", [T, D], F32, kind="ExternalOutput")
    dbg = "ExternalOutput" if (debug or eo_scratch) else "Internal"
    x1_d = nc.dram_tensor("x1sc", [T, D], F32, kind=dbg)
    z_d = nc.dram_tensor("zsc", [T, D], F32, kind=dbg)
    za_d = nc.dram_tensor("zasc", [T, D], F32, kind=dbg)
    zsh_d = nc.dram_tensor("zshsc", [T + TL, D], BF16, kind="Internal")
    zash_d = nc.dram_tensor("zashsc", [T + TL, DI], BF16, kind="Internal")
    z2_d = nc.dram_tensor("z2sc", [T, D], F32, kind=dbg)
    if debug:
        tapH_d = nc.dram_tensor("tapH", [T, D], BF16, kind="ExternalOutput")
        tapZ_d = nc.dram_tensor("tapZ", [T, D], F32, kind="ExternalOutput")
        tapRot_d = nc.dram_tensor("tapRot", [T, D], F32, kind="ExternalOutput")
        tapP_d = nc.dram_tensor("tapP", [T, 640], BF16, kind="ExternalOutput")
        tapPT_d = nc.dram_tensor("tapPT", [T, 640], BF16, kind="ExternalOutput")
        tapSM_d = nc.dram_tensor("tapSM", [T, 2], F32, kind="ExternalOutput")
        tapV_d = nc.dram_tensor("tapV", [T, DI], BF16, kind="ExternalOutput")
        tapW_d = nc.dram_tensor("tapW", [128, 8 * 3 * DI], BF16, kind="ExternalOutput")
        tapL2_d = nc.dram_tensor("tapL2", [T, 16], F32, kind="ExternalOutput")
        tapL3_d = nc.dram_tensor("tapL3", [T, 16], F32, kind="ExternalOutput")
        tapSQ_d = nc.dram_tensor("tapSQ", [T, D], F32, kind="ExternalOutput")
        tapQR_d = nc.dram_tensor("tapQR", [T, D], F32, kind="ExternalOutput")
        tapLN2_d = nc.dram_tensor("tapLN2", [128, NT, 2], F32, kind="ExternalOutput")
        tapH2_d = nc.dram_tensor("tapH2", [T, D], BF16, kind="ExternalOutput")
        tapH2F_d = nc.dram_tensor("tapH2F", [T // 2, 8, 256], BF16, kind="ExternalOutput")
        tapPA_d = nc.dram_tensor("tapPA", [T, 512], F32, kind="ExternalOutput")
        tapU_d = nc.dram_tensor("tapU", [T, 256], BF16, kind="ExternalOutput")
        tapT2_d = nc.dram_tensor("tapT2", [T, D], F32, kind="ExternalOutput")
        tapT3_d = nc.dram_tensor("tapT3", [T, D], F32, kind="ExternalOutput")
        tapS_d = nc.dram_tensor("tapS", [T, 8], F32, kind="ExternalOutput")
        tapQK_d = nc.dram_tensor("tapQK", [T, 2 * DI], F32, kind="ExternalOutput")
        tapR_d = nc.dram_tensor("tapR", [T, 2 * DI], BF16, kind="ExternalOutput")
        tapO_d = nc.dram_tensor("tapO", [T, DI], BF16, kind="ExternalOutput")
        tapF_d = nc.dram_tensor("tapF", [T, D], BF16, kind="ExternalOutput")

    top = ExitStack()

    def sbuf(name, shape, dt=F32, es=None):
        return (es or top).enter_context(nc.sbuf_tensor(name, shape, dt))

    def psum(name, shape, dt=F32, es=None):
        return (es or top).enter_context(nc.psum_tensor(name, shape, dt))

    mCAU_s = sbuf("mCAUs", [TL, TL])
    mSC_s = sbuf("mSC", [TL, 2 * TL])
    mUC_s = sbuf("mUC", [TL, 2 * TL])
    idb_s = sbuf("idbs", [TL, TL], BF16)
    shm_s = sbuf("shms", [TL, TL])
    pk_s = sbuf("pks", [TL, TL])
    qsb_s = sbuf("qsb", [128, 2 * DI])
    lngb_s = sbuf("lngb", [128, D])
    lnbb_s = sbuf("lnbb", [128, D])
    ln2_s = sbuf("ln2", [128, NT, 2])
    sq_s = sbuf("sqscr", [128, D])
    b1_s = sbuf("b1s", [128, 44])
    cv_s = sbuf("cvs", [128, 6])
    zrow_s = sbuf("zrow", [1, D], BF16)
    stLN = sbuf("stLN", [128, 8])
    stSM = sbuf("stSM", [128, 2])
    stL2 = sbuf("stL2", [128, 16])
    stL2b = sbuf("stL2b", [128, 16])
    stSH = sbuf("stSH", [128, 6])
    stSHp = sbuf("stSHp", [128, 2])

    bufs = {}

    def B(name):
        if name not in bufs:
            b = Buf()
            b.r = list(S.barrier)
            bufs[name] = b
        return bufs[name]

    bLN, bSM, bL2, bSH, bSHp = B('stLN'), B('stSM'), B('stL2'), B('stSH'), B('stSHp')
    bSQ, bLn2, bB1 = B('sqscr'), B('ln2'), B('b1')

    def ld(q, dst_ap, src_ap, wbuf, deps=()):
        return S.op(q, lambda e, d_=dst_ap, s_=src_ap: e.dma_start(out=d_, in_=s_),
                    deps=deps, writes=(wbuf,))

    ld('Y', cv_s[:], cv_d.ap(), B('cv'))
    S.op('V', lambda e: e.memset(sq_s[0:1, 0:DI], 0.0), writes=(B('sqz'),))
    S.op('V', lambda e: e.memset(zrow_s[:], 0.0), writes=(B('zrow'),))
    S.op('SD', lambda e: e.dma_start(out=zsh_d.ap()[0:1, DI:D], in_=zrow_s[0:1, 0:DI]),
         reads=(B('zrow'),), writes=(B('zsh@-1'),))
    S.op('SD', lambda e: e.dma_start(out=zash_d.ap()[0:1, :], in_=zrow_s[0:1, 0:DI]),
         reads=(B('zrow'),), writes=(B('zash@-1'),))
    ld('Y', mCAU_s[:], mCAU_d.ap(), B('mCAU'))
    ld('Y', mSC_s[:], mSC_d.ap(), B('mSC'))
    ld('Y', mUC_s[:], mUC_d.ap(), B('mUC'))
    ld('Y', idb_s[:], idb_d.ap(), B('idb'))
    ld('Y', shm_s[:], shm_d.ap(), B('shm'))
    ld('Y', pk_s[:], pk_d.ap(), B('pk'))

    def ln_stats(srcfn, src_bufs):
        S.op('V', lambda e: srcfn(e, 'sum'), reads=src_bufs, writes=(bLN,))
        S.op('S', lambda e: srcfn(e, 'sq'), reads=src_bufs, writes=(bSQ, bLN))
        S.op('V', lambda e: e.tensor_scalar(
            out=stLN[:, 2:3], in0=stLN[:, 0:1], scalar1=stLN[:, 0:1],
            scalar2=1.0 / D, op0=ALU.mult, op1=ALU.mult),
            fence_reads=(bLN,), writes=(bLN,))
        S.op('V', lambda e: e.tensor_scalar(
            out=stLN[:, 3:4], in0=stLN[:, 1:2], scalar1=stLN[:, 2:3],
            scalar2=1.0 / D, op0=ALU.subtract, op1=ALU.mult),
            fence_reads=(bLN,), writes=(bLN,))
        S.op('S', lambda e: e.activation(stLN[:, 3:4], stLN[:, 3:4], AF.Ln, bias=cv_s[:, 0:1]),
             reads=(bLN, B('cv')), writes=(bLN,))
        S.op('S', lambda e: e.activation(stLN[:, 4:5], stLN[:, 3:4], AF.Exp, scale=cv_s[:, 1:2]),
             reads=(bLN, B('cv')), writes=(bLN,))
        S.op('V', lambda e: e.tensor_scalar(
            out=stLN[:, 5:6], in0=stLN[:, 0:1], scalar1=stLN[:, 4:5],
            scalar2=-1.0 / D, op0=ALU.mult, op1=ALU.mult),
            fence_reads=(bLN,), writes=(bLN,))

    def attn_pass(kind, l=None):
        S.pass_start()
        es = ExitStack()
        inner = kind == 'B1'
        WIN = 4 if inner else 2
        tag = f"{kind}{l}"
        wqkv_s = sbuf(f"wqkv{tag}", [128, 8, 3 * DI], BF16, es)
        wo_s = sbuf(f"wo{tag}", [128, 4, D], BF16, es)
        kfm = sbuf(f"kfm{tag}", [128, 4, T], BF16, es)
        vst = sbuf(f"vst{tag}", [128, NT, H, HD], BF16, es)
        ztile = [sbuf(f"zt{tag}{i}", [128, D], F32, es) for i in range(3)]
        hbuf = [sbuf(f"hb{tag}{i}", [128, D], BF16, es) for i in range(3)]
        ctab = [sbuf(f"ct{tag}{i}", [128, D], F32, es) for i in range(3)]
        stab = [sbuf(f"sn{tag}{i}", [128, D], F32, es) for i in range(3)]
        hfm = sbuf(f"hfm{tag}", [128, 8, TL], BF16, es)
        qk = sbuf(f"qk{tag}", [128, D], F32, es)
        rot = sbuf(f"rot{tag}", [128, D], F32, es)
        qkr = sbuf(f"qkr{tag}", [128, D], BF16, es)
        qfm = [sbuf(f"qfm{tag}{i}", [128, 4, TL], BF16, es) for i in range(2)]
        Pb = [sbuf(f"P{tag}{i}", [128, (WIN + 1) * TL], BF16, es) for i in range(2)]
        Pn = [sbuf(f"Pn{tag}{i}", [128, (WIN + 1) * TL], BF16, es) for i in range(2)]
        stSM2 = [sbuf(f"sm{tag}{i}", [128, 2], F32, es) for i in range(2)]
        PTs = [sbuf(f"PT{tag}{i}", [128, (WIN + 1) * TL], BF16, es) for i in range(2)]
        ofm = [sbuf(f"ofm{tag}{i}", [128, 4, TL], BF16, es) for i in range(2)]
        zares = [sbuf(f"za{tag}{i}", [128, D], F32, es) for i in range(2)]
        lnt = sbuf(f"lnt{tag}", [128, D], F32, es)
        lnb = sbuf(f"lnbf{tag}", [128, D], BF16, es)
        sqP = sbuf(f"sqP{tag}", [128, D], F32, es)
        sqE = sbuf(f"sqE{tag}", [128, D], F32, es)
        stP = sbuf(f"stP{tag}", [128, 8], F32, es)
        stE = sbuf(f"stE{tag}", [128, 8], F32, es)
        stZ = sbuf(f"stZ{tag}", [128, 8], F32, es)
        stL2a = sbuf(f"stL2a{tag}", [128, 16], F32, es)
        stL2b = sbuf(f"stL2b{tag}", [128, 16], F32, es)
        pP1 = psum(f"pP1{tag}", [128, 1024], F32, es)
        pPtr = psum(f"pPtr{tag}", [128, 1024], BF16, es)
        pA0 = psum(f"pA0{tag}", [128, 1024], F32, es)
        pA1 = psum(f"pA1{tag}", [128, 1024], F32, es)
        pO = psum(f"pO{tag}", [128, 128], F32, es)
        pPT0 = pPtr
        pA = [pA0, pA1]
        bpP1 = B(f"pP1{tag}")
        bpPtr = B(f"pPtr{tag}")
        bpA = [B(f"pA0{tag}"), B(f"pA1{tag}")]
        bpPT0, bpO = bpPtr, B(f"pO{tag}")
        bkfm, bvst = B(f"kfm{tag}"), B(f"vst{tag}")
        bwqkv, bwo = B(f"wqkv{tag}"), B(f"wo{tag}")
        bhfm, bqk = B(f"hfm{tag}"), B(f"qk{tag}")
        brot = B(f"rot{tag}")
        bqkr = B(f"qkr{tag}")
        bqfm = [B(f"qfm{tag}0"), B(f"qfm{tag}1")]
        bP = [B(f"P{tag}0"), B(f"P{tag}1")]
        bPn = [B(f"Pn{tag}0"), B(f"Pn{tag}1")]
        bSM2 = [B(f"sm{tag}0"), B(f"sm{tag}1")]
        bPT = [B(f"PT{tag}0"), B(f"PT{tag}1")]
        bofm = [B(f"ofm{tag}0"), B(f"ofm{tag}1")]
        blnt, blnb = B(f"lnt{tag}"), B(f"lnbf{tag}")
        bsqP, bsqE = B(f"sqP{tag}"), B(f"sqE{tag}")
        bstP, bstE, bstZ = B(f"stP{tag}"), B(f"stE{tag}"), B(f"stZ{tag}")
        bL2a, bL2b = B(f"stL2a{tag}"), B(f"stL2b{tag}")
        bzt = [B(f"zt{tag}{i}") for i in range(3)]
        bhb = [B(f"hb{tag}{i}") for i in range(3)]
        bct = [B(f"ct{tag}{i}") for i in range(3)]
        bsn = [B(f"sn{tag}{i}") for i in range(3)]
        bza = [B(f"za{tag}{i}") for i in range(2)]

        src_d = x0_d if kind == 'A' else z_d
        ct_d = oc_d if kind == 'A' else ic_d
        st_d = os_d if kind == 'A' else is_d

        wq_src = wqkvO_d if kind == 'A' else wqkvI_d[l]
        wo_src = woutO_d if kind == 'A' else woI_d[l]
        ld('GD', wqkv_s[:], wq_src.ap().rearrange("(k p) n -> p k n", p=128), bwqkv)
        ld('GD', wo_s[:], wo_src.ap().rearrange("(k p) n -> p k n", p=128), bwo)
        if inner:
            ld('Y', qsb_s[:, 0:DI], qs_d[l].ap().partition_broadcast(128), B('qsb'))
            ld('Y', qsb_s[:, DI:2 * DI], ks_d[l].ap().partition_broadcast(128), B('qsb'))

        def ln_chain(st, bst, sumfn, sqfn, sb, bsb, extradep=()):
            S.op('V', sumfn, deps=list(extradep), writes=(bst,))
            S.op('S', sqfn, deps=list(extradep), writes=(bsb, bst))
            S.op('V', lambda e: e.tensor_scalar(
                out=st[:, 2:3], in0=st[:, 0:1], scalar1=st[:, 0:1],
                scalar2=1.0 / (D * D), op0=ALU.mult, op1=ALU.mult),
                fence_reads=(bst,), writes=(bst,))
            S.op('V', lambda e: e.tensor_scalar(
                out=st[:, 3:4], in0=st[:, 1:2], scalar1=st[:, 2:3],
                scalar2=1.0, op0=ALU.subtract, op1=ALU.mult),
                fence_reads=(bst,), writes=(bst,))
            S.op('S', lambda e: e.activation(st[:, 3:4], st[:, 3:4], AF.Ln,
                                             bias=cv_s[:, 0:1]),
                 reads=(bst, B('cv')), writes=(bst,))
            S.op('S', lambda e: e.activation(st[:, 4:5], st[:, 3:4], AF.Exp,
                                             scale=cv_s[:, 1:2]),
                 fence_reads=(bst,), writes=(bst,))
            S.op('V', lambda e: e.tensor_scalar(
                out=st[:, 5:6], in0=st[:, 0:1], scalar1=st[:, 4:5],
                scalar2=-1.0 / D, op0=ALU.mult, op1=ALU.mult),
                fence_reads=(bst,), writes=(bst,))

        def tload(t):
            i4 = t % 3
            S.op('Y', lambda e, t_=t: e.dma_start(
                out=ztile[i4][:], in_=src_d.ap()[t_ * TL:(t_ + 1) * TL, :]),
                deps=[B(f"{sname}@{t}").w], writes=(bzt[i4],))
            ld('Y', ctab[i4][:], ct_d.ap()[t * TL:(t + 1) * TL, :], bct[i4])
            ld('Y', stab[i4][:], st_d.ap()[t * TL:(t + 1) * TL, :], bsn[i4])
            if inner:
                S.op('Y', lambda e, t_=t, hb=hbuf[i4]: e.dma_start(
                    out=hb[:], in_=zsh_d.ap()[t_ * TL:(t_ + 1) * TL, :]),
                    deps=[B(f"zsh@{t - 1}").w, B(f"zsh@{t}").w], writes=(bhb[i4],))

        sname = 'x0' if kind == 'A' else 'zs'

        def stage_p1(t):
            if kind != 'A':
                return
            zt, bz = ztile[t % 3], bzt[t % 3]
            h, bh = hbuf[t % 3], bhb[t % 3]

            def sumfn(e, z_=zt):
                return e.reduce_sum(stP[:, 0:1], z_[:], axis=AX.X)

            def sqfn(e, z_=zt):
                return e.activation(sqP[:], z_[:], AF.Square,
                                    scale=cv_s[:, 3:4], accum_out=stP[:, 1:2])
            ln_chain(stP, bstP, sumfn, sqfn, sqP, bsqP, extradep=[bz.w])
            S.op('V', lambda e, z_=zt, h_=h: e.tensor_scalar(
                out=h_[:], in0=z_[:], scalar1=stP[:, 4:5], scalar2=stP[:, 5:6],
                op0=ALU.mult, op1=ALU.add), reads=(bz,), fence_reads=(bstP,),
                writes=(bh,))

        def stage_p(t):
            zt, bz = ztile[t % 3], bzt[t % 3]
            cb, bcb = ctab[t % 3], bct[t % 3]
            sn, bsb = stab[t % 3], bsn[t % 3]
            h, bh = hbuf[t % 3], bhb[t % 3]

            if debug and kind == tapkind and 'H' in tapsel:
                S.op('SD', lambda e, t_=t, h_=h: e.dma_start(
                    out=tapH_d.ap()[t_ * TL:(t_ + 1) * TL, :], in_=h_[:]), reads=(bh,))
            def tr_h(e, h_=h):
                for c in range(8):
                    ins = e.transpose(pPtr[:, c * TL:(c + 1) * TL],
                                      h_[:, c * TL:(c + 1) * TL], idb_s[:])
                return ins
            S.op('T', tr_h, deps=[B('idb').w], reads=(bh,), writes=(bpPtr,))
            S.op('V', lambda e: e.tensor_copy(
                hfm[:].rearrange("p a b -> p (a b)"), pPtr[:]),
                reads=(bpPtr,), writes=(bhfm,))

            def mm_qkv(e, c0, c1, dst, o0, o1):
                for k in range(8):
                    ins = e.matmul(dst[:, o0:o1], hfm[:, k, :],
                                   wqkv_s[:, k, c0:c1], start=(k == 0), stop=(k == 7))
                return ins
            S.op('T', lambda e: mm_qkv(e, 0, DI, pP1, 0, DI),
                 reads=(bhfm, bwqkv), writes=(bpP1,))
            S.op('T', lambda e: mm_qkv(e, DI, 2 * DI, pP1, DI, 2 * DI),
                 reads=(bhfm, bwqkv), writes=(bpP1,))
            S.op('V', lambda e: e.tensor_copy(qk[:], pP1[:]),
                 reads=(bpP1,), writes=(bqk,))
            S.op('T', lambda e: mm_qkv(e, 2 * DI, 3 * DI, pP1, 0, DI),
                 reads=(bhfm, bwqkv), writes=(bpP1,))
            S.op('S', lambda e, t_=t: e.copy(
                vst[:, t_, :, :].rearrange("p a b -> p (a b)"), pP1[:, 0:DI]),
                reads=(bpP1,), writes=(bvst,))

            if inner:
                S.op('V', lambda e: e.tensor_mul(sqP[:], qk[:], qk[:]),
                     reads=(bqk,), writes=(bsqP,))
                S.op('V', lambda e: e.reduce_sum(
                    stL2a[:], sqP[:].rearrange("p (s c) -> p s c", s=16), axis=AX.X),
                    fence_reads=(bsqP,), writes=(bL2a,))
                S.op('V', lambda e: e.tensor_scalar(
                    out=stL2b[:], in0=stL2a[:], scalar1=1e-24, scalar2=None,
                    op0=ALU.max), fence_reads=(bL2a,), writes=(bL2b,))
                S.op('S', lambda e: e.activation(stL2a[:], stL2b[:], AF.Ln),
                     reads=(bL2b,), writes=(bL2a,))
                S.op('S', lambda e: e.activation(stL2b[:], stL2a[:], AF.Exp,
                                                 scale=cv_s[:, 1:2]),
                     reads=(bL2a,), writes=(bL2b,))

                def apply_rstd(e):
                    for i in range(16):
                        ins = e.tensor_scalar(
                            out=qk[:, i * 64:(i + 1) * 64],
                            in0=qk[:, i * 64:(i + 1) * 64],
                            scalar1=stL2b[:, i:i + 1], scalar2=None, op0=ALU.mult)
                    return ins
                S.op('V', apply_rstd, fence_reads=(bL2b,), writes=(bqk,))
                S.op('V', lambda e: e.tensor_mul(qk[:], qk[:], qsb_s[:]),
                     reads=(B('qsb'), bqk), writes=(bqk,))

            S.op('S', lambda e: e.mul(
                rot[:].rearrange("p (s c) -> p s c", s=16)[:, :, 0:32],
                qk[:].rearrange("p (s c) -> p s c", s=16)[:, :, 32:64], cv_s[:, 2:3]),
                reads=(bqk, B('cv')), writes=(brot,))
            S.op('S', lambda e: e.copy(
                rot[:].rearrange("p (s c) -> p s c", s=16)[:, :, 32:64],
                qk[:].rearrange("p (s c) -> p s c", s=16)[:, :, 0:32]),
                reads=(bqk,), writes=(brot,))
            S.op('V', lambda e, c_=cb: e.tensor_mul(qk[:], qk[:], c_[:]),
                 reads=(bcb,), writes=(bqk,))
            S.op('G', lambda e, s_=sn: e.tensor_mul(rot[:], rot[:], s_[:]),
                 reads=(bsb,), writes=(brot,))
            S.op('V', lambda e: e.tensor_add(qkr[:], qk[:], rot[:]),
                 reads=(bqk, brot), writes=(bqkr,))

            if debug and kind == tapkind and 'Q' in tapsel:
                S.op('SD', lambda e, t_=t: e.dma_start(
                    out=tapQK_d.ap()[t_ * TL:(t_ + 1) * TL, :], in_=qk[:]), reads=(bqk,))
            if debug and kind == tapkind and 'R' in tapsel:
                S.op('SD', lambda e, t_=t: e.dma_start(
                    out=tapR_d.ap()[t_ * TL:(t_ + 1) * TL, :], in_=qkr[:]), reads=(bqkr,))
            def tr_qk(e):
                for c in range(8):
                    ins = e.transpose(pPtr[:, c * TL:(c + 1) * TL],
                                      qkr[:, c * TL:(c + 1) * TL], idb_s[:])
                return ins
            S.op('T', tr_qk, deps=[B('idb').w], reads=(bqkr,), writes=(bpPtr,))
            qf = qfm[t % 2]
            S.op('V', lambda e, q_=qf: e.tensor_copy(
                q_[:].rearrange("p a b -> p (a b)"), pPtr[:, 0:DI]),
                reads=(bpPtr,), writes=(bqfm[t % 2],))
            S.op('V', lambda e, t_=t: e.tensor_copy(
                kfm[:, :, t_ * TL:(t_ + 1) * TL],
                pPtr[:, DI:2 * DI].rearrange("p (a b) -> p a b", a=4)),
                reads=(bpPtr,), writes=(bkfm,))

        def stage_a(t):
            t0 = max(0, t - WIN)
            nck = t - t0 + 1
            nk = nck * TL
            own = (t - t0) * TL
            qf, bqf = qfm[t % 2], bqfm[t % 2]
            of, bof = ofm[t % 2], bofm[t % 2]

            def mm_scores(e, hh):
                p_, s_ = hh // 2, hh % 2
                for o in range(0, nk, 512):
                    o2 = min(nk, o + 512)
                    ins = e.matmul(pA[hh % 2][:, o:o2],
                                   qf[s_ * 64:(s_ + 1) * 64, p_, :],
                                   kfm[s_ * 64:(s_ + 1) * 64, p_,
                                       t0 * TL + o:t0 * TL + o2],
                                   start=True, stop=True)
                return ins
            S.op('T', lambda e: mm_scores(e, 0), reads=(bqf, bkfm), writes=(bpA[0],))
            S.op('T', lambda e: mm_scores(e, 1), reads=(bqf, bkfm), writes=(bpA[1],))
            for hh in range(H):
                pg = hh % 2
                if t >= WIN:
                    medge = mSC_s if kind == 'A' else mUC_s
                    S.op('V', lambda e, g=pg, m=medge: e.tensor_tensor(
                        out=bass.AP(pA[g], 0, [[1024, 128], [own, 2], [1, TL]]),
                        in0=bass.AP(pA[g], 0, [[1024, 128], [own, 2], [1, TL]]),
                        in1=m[:].rearrange("p (c f) -> p c f", c=2), op=ALU.add),
                        deps=[B('mSC').w, B('mUC').w],
                        reads=(bpA[pg],), writes=(bpA[pg],))
                else:
                    S.op('V', lambda e, g=pg: e.tensor_tensor(
                        out=pA[g][:, own:own + TL], in0=pA[g][:, own:own + TL],
                        in1=mCAU_s[:], op=ALU.add),
                        deps=[B('mCAU').w], reads=(bpA[pg],), writes=(bpA[pg],))
                sm, bsm = stSM2[pg], bSM2[pg]
                pb_, bpb = Pb[pg], bP[pg]
                pn_, bpn = Pn[pg], bPn[pg]
                S.op('S', lambda e, g=pg, s_=sm, p_=pb_: e.activation(
                    p_[:, 0:nk], pA[g][:, 0:nk], AF.Exp,
                    accum_out=s_[:, 1:2]),
                    reads=(bpA[pg],), writes=(bpb, bsm))
                if hh + 2 < H:
                    S.op('T', lambda e, n=hh + 2: mm_scores(e, n),
                         reads=(bqf, bkfm), writes=(bpA[pg],))
                S.op('V', lambda e, s_=sm: e.reciprocal(s_[:, 0:1], s_[:, 1:2]),
                     reads=(bsm,), writes=(bsm,))
                S.op('S', lambda e, s_=sm, p_=pb_, n_=pn_: e.activation(
                    n_[:, 0:nk], p_[:, 0:nk], AF.Copy, scale=s_[:, 0:1]),
                    reads=(bpb, bsm), writes=(bpn,))

                if debug and kind == tapkind and hh == 0 and 'P' in tapsel:
                    S.op('SD', lambda e, t_=t, n_=nk, q_=pn_: e.dma_start(
                        out=tapP_d.ap()[t_ * TL:(t_ + 1) * TL, 0:n_], in_=q_[:, 0:n_]),
                        reads=(bpn,))
                def tr_p(e, q_=pn_):
                    for c in range(nck):
                        ins = e.transpose(pPT0[:, c * TL:(c + 1) * TL],
                                          q_[:, c * TL:(c + 1) * TL], idb_s[:])
                    return ins
                S.op('T', tr_p, reads=(bpn,), writes=(bpPT0,))
                S.op('S', lambda e, g=pg: e.copy(PTs[g][:, 0:nk], pPT0[:, 0:nk]),
                     reads=(bpPT0,), writes=(bPT[pg],))

                def mm_av(e, g=pg, h_=hh):
                    for c in range(nck):
                        ins = e.matmul(pO[0:64, :], vst[:, t0 + c, h_, :],
                                       PTs[g][:, c * TL:(c + 1) * TL],
                                       start=(c == 0), stop=(c == nck - 1))
                    return ins
                S.op('T', mm_av, reads=(bPT[pg], bvst), writes=(bpO,))
                S.op('V', lambda e, o_=of, p_=hh // 2, s_=hh % 2: e.tensor_copy(
                    o_[s_ * 64:(s_ + 1) * 64, p_, :], pO[0:64, :]),
                    reads=(bpO,), writes=(bof,))

        def stage_e(t):
            zt, bz = ztile[t % 3], bzt[t % 3]
            of, bof = ofm[t % 2], bofm[t % 2]
            za, bzr = zares[t % 2], bza[t % 2]

            if debug and kind == tapkind and 'O' in tapsel:
                S.op('SD', lambda e, t_=t, o_=of: e.dma_start(
                    out=tapO_d.ap()[t_ * TL:(t_ + 1) * TL, :],
                    in_=o_[:].rearrange("p a b -> p (a b)")), reads=(bof,))
            def mm_wo(e, o0, o1):
                for c in range(4):
                    ins = e.matmul(pA0[:, o0:o1], of[:, c, :], wo_s[:, c, o0:o1],
                                   start=(c == 0), stop=(c == 3))
                return ins
            S.op('T', lambda e: mm_wo(e, 0, DI), reads=(bof, bwo), writes=(bpA[0],))
            S.op('T', lambda e: mm_wo(e, DI, D), reads=(bof, bwo), writes=(bpA[0],))

            if kind == 'A':
                def sumfn(e):
                    return e.reduce_sum(stE[:, 0:1], pA0[:], axis=AX.X)

                def sqfn(e):
                    return e.activation(sqE[:], pA0[:], AF.Square,
                                        scale=cv_s[:, 3:4], accum_out=stE[:, 1:2])
                ln_chain(stE, bstE, sumfn, sqfn, sqE, bsqE, extradep=[bpA[0].w])
                S.op('V', lambda e: e.tensor_scalar(
                    out=lnt[:], in0=pA0[:], scalar1=stE[:, 4:5], scalar2=stE[:, 5:6],
                    op0=ALU.mult, op1=ALU.add), reads=(bpA[0],), fence_reads=(bstE,),
                    writes=(blnt,))
                S.op('V', lambda e, z_=zt: e.tensor_add(za[:], lnt[:], z_[:]),
                     reads=(blnt, bz), writes=(bzr,))
                S.op('SD', lambda e, t_=t: e.dma_start(
                    out=x1_d.ap()[t_ * TL:(t_ + 1) * TL, :], in_=za[:]),
                    reads=(bzr,), writes=(B(f"x1@{t}"),))

                def sumfn2(e):
                    return e.reduce_sum(stZ[:, 0:1], za[:], axis=AX.X)

                def sqfn2(e):
                    return e.activation(sqE[:], za[:], AF.Square,
                                        scale=cv_s[:, 3:4], accum_out=stZ[:, 1:2])
                ln_chain(stZ, bstZ, sumfn2, sqfn2, sqE, bsqE, extradep=[bzr.w])
                S.op('V', lambda e: e.tensor_scalar(
                    out=lnt[:], in0=za[:], scalar1=stZ[:, 4:5], scalar2=stZ[:, 5:6],
                    op0=ALU.mult, op1=ALU.add), reads=(bzr,), fence_reads=(bstZ,),
                    writes=(blnt,))
                S.op('V', lambda e: e.tensor_copy(lnb[:], lnt[:]),
                     reads=(blnt,), writes=(blnb,))
                S.op('SD', lambda e, t_=t: e.dma_start(
                    out=z_d.ap()[t_ * TL:(t_ + 1) * TL, :], in_=lnt[:]),
                    reads=(blnt,), writes=(B(f"zs@{t}"),))
                S.op('SD', lambda e, t_=t: e.dma_start(
                    out=zsh_d.ap()[t_ * TL:(t_ + 1) * TL, 0:DI],
                    in_=lnb[:, 0:DI]), reads=(blnb,), writes=(B(f"zsh@{t}"),))
                S.op('SD', lambda e, t_=t: e.dma_start(
                    out=zsh_d.ap()[t_ * TL + 1:(t_ + 1) * TL + 1, DI:D],
                    in_=lnb[:, DI:D]), reads=(blnb,), writes=(B(f"zsh@{t}"),))
            else:
                S.op('V', lambda e, z_=zt: e.tensor_add(za[:], pA0[:], z_[:]),
                     reads=(bpA[0], bz), writes=(bzr,))
                S.op('SD', lambda e, t_=t: e.dma_start(
                    out=za_d.ap()[t_ * TL:(t_ + 1) * TL, :], in_=za[:]),
                    reads=(bzr,), writes=(B(f"za@{t}"),))
                S.op('V', lambda e: e.tensor_copy(lnb[:, 0:DI], za[:, DI:D]),
                     reads=(bzr,), writes=(blnb,))
                S.op('SD', lambda e, t_=t: e.dma_start(
                    out=zash_d.ap()[t_ * TL + 1:(t_ + 1) * TL + 1, :],
                    in_=lnb[:, 0:DI]), reads=(blnb,), writes=(B(f"zash@{t}"),))
                S.op('V', lambda e: e.reduce_sum(stE[:, 0:1], za[:, 0:DI], axis=AX.X),
                     reads=(bzr,), writes=(bstE,))
                S.op('V', lambda e: e.reduce_sum(stE[:, 2:3], za[:, DI:D], axis=AX.X),
                     reads=(bzr,), writes=(bstE,))
                S.op('S', lambda e: e.activation(sqE[:, 0:DI], za[:, 0:DI], AF.Square,
                                                 accum_out=stE[:, 1:2]),
                     reads=(bzr,), writes=(bsqE, bstE))
                S.op('S', lambda e: e.activation(sqE[:, DI:D], za[:, DI:D], AF.Square,
                                                 accum_out=stE[:, 3:4]),
                     reads=(bzr,), writes=(bsqE, bstE))

                def mm_shift(e, t_=t):
                    ins = e.matmul(pO[:, 0:2], shm_s[:], stE[:, 2:4],
                                   start=True, stop=(t_ == 0))
                    if t_ > 0:
                        ins = e.matmul(pO[:, 0:2], pk_s[:], stSHp[:],
                                       start=False, stop=True)
                    return ins
                S.op('T', mm_shift, deps=[B('shm').w, B('pk').w],
                     reads=(bstE, bSHp), writes=(bpO,))
                S.op('V', lambda e: e.tensor_copy(stSHp[:], stE[:, 2:4]),
                     reads=(bstE,), writes=(bSHp,))
                S.op('V', lambda e: e.tensor_add(stSH[:, 4:6], stE[:, 0:2],
                                                 pO[:, 0:2]),
                     reads=(bstE, bpO), writes=(bSH,))
                S.op('V', lambda e: e.tensor_scalar(
                    out=stSH[:, 0:1], in0=stSH[:, 4:5], scalar1=stSH[:, 4:5],
                    scalar2=1.0 / D, op0=ALU.mult, op1=ALU.mult),
                    fence_reads=(bSH,), writes=(bSH,))
                S.op('V', lambda e: e.tensor_scalar(
                    out=stSH[:, 1:2], in0=stSH[:, 5:6], scalar1=stSH[:, 0:1],
                    scalar2=1.0 / D, op0=ALU.subtract, op1=ALU.mult),
                    fence_reads=(bSH,), writes=(bSH,))
                S.op('S', lambda e: e.activation(stSH[:, 1:2], stSH[:, 1:2], AF.Ln,
                                                 bias=cv_s[:, 0:1]),
                     reads=(bSH, B('cv')), writes=(bSH,))
                S.op('S', lambda e, t_=t: e.activation(
                    ln2_s[:, t_, 0:1], stSH[:, 1:2], AF.Exp, scale=cv_s[:, 1:2]),
                    reads=(bSH,), writes=(bLn2,))
                S.op('V', lambda e, t_=t: e.tensor_scalar(
                    out=ln2_s[:, t_, 1:2], in0=stSH[:, 4:5],
                    scalar1=ln2_s[:, t_, 0:1], scalar2=-1.0 / D,
                    op0=ALU.mult, op1=ALU.mult), reads=(bSH,),
                    fence_reads=(bLn2,), writes=(bLn2,))

        tload(0)
        tload(1)
        tload(2)
        stage_p1(0)
        stage_p1(1)
        stage_p1(2)
        stage_p(0)
        stage_p(1)
        for t in range(NT):
            stage_a(t)
            stage_e(t)
            if t + 3 < NT:
                tload(t + 3)
                stage_p1(t + 3)
            if t + 2 < NT:
                stage_p(t + 2)
        es.close()


    def ffn_pass(l, last):
        S.pass_start()
        es = ExitStack()
        tag = f"F{l}"
        PARTS = [(0, 6), (6, 11), (11, 16), (16, 22)]
        PLO = []
        w1sp = []
        for i, (c0, c1) in enumerate(PARTS):
            lo = _pair_geom(c0)[1]
            wl, ol = _pair_geom(c1 - 1)
            hi = ol + 2 * wl
            PLO.append((lo, hi))
            w1sp.append(sbuf(f"w1s{tag}p{i}", [128, 8, hi - lo], BF16, es))
        w2s = sbuf(f"w2s{tag}", [128, NPAIR, D], BF16, es)
        zag = [sbuf(f"zag{tag}{i}", [128, 2 * D], F32, es) for i in range(3)]
        zsh = [sbuf(f"zsh{tag}{i}", [128, 2 * DI], BF16, es) for i in range(3)]
        h2 = sbuf(f"h2{tag}", [128, 2 * D], BF16, es)
        h2fm = sbuf(f"h2fm{tag}", [128, 8, 256], BF16, es)
        lt = sbuf(f"lt{tag}", [128, D], F32, es)
        gg = sbuf(f"gg{tag}", [128, 256], BF16, es)
        ub = [sbuf(f"ub{tag}{i}", [128, 256], BF16, es) for i in range(2)]
        u21 = sbuf(f"u21{tag}", [43, 256], BF16, es)
        zn = [sbuf(f"zn{tag}0", [128, 2 * D], F32, es)] * 2
        znb16 = None if last else sbuf(f"znb16{tag}", [128, 2 * D], BF16, es)
        pZ0 = psum(f"pZ0{tag}", [128, 1024], F32, es)
        pZ1 = psum(f"pZ1{tag}", [128, 1024], F32, es)
        pA = psum(f"pA{tag}", [128, 512], F32, es)
        pB = psum(f"pB{tag}", [128, 512], F32, es)
        pT2 = psum(f"pT2{tag}", [128, 2048], BF16, es)
        pZ = [pZ0, pZ1]
        pAG = [pA, pB]
        bpZ = [B(f"pZ0{tag}"), B(f"pZ1{tag}")]
        bpAG = [B(f"pA{tag}"), B(f"pB{tag}")]
        bpT2 = B(f"pT2{tag}")
        bzag = [B(f"zag{tag}{i}") for i in range(3)]
        bzsh = [B(f"zsh{tag}{i}") for i in range(3)]
        bh2, bh2fm, blt = B(f"h2{tag}"), B(f"h2fm{tag}"), B(f"lt{tag}")
        bgg = B(f"gg{tag}")
        bub = [B(f"ub{tag}{i}") for i in range(2)]
        bu21 = B(f"u21{tag}")
        bzn = [B(f"zn{tag}0")] * 2
        bw1, bw2 = B(f"w1s{tag}"), B(f"w2s{tag}")

        dst_d = z2_d if last else z_d
        partof = [next(i for i, (c0, c1) in enumerate(PARTS) if c0 <= c < c1)
                  for c in range(NPAIR)]
        w1deps, w2deps = [], []
        for i, (c0, c1) in enumerate(PARTS):
            lo, hi = PLO[i]
            import os as _os5
            if _os5.environ.get('NOW1'):
                d = S.op('V', lambda e, i_=i: e.memset(w1sp[i_][:], 0.0), writes=(bw1,))
            else:
                d = S.op('GD', lambda e, i_=i: e.dma_start(
                    out=w1sp[i_][:],
                    in_=w1p_d[l][i_].ap().rearrange("(k p) n -> p k n", p=128)),
                    writes=(bw1,))
            w1deps.append(d)
            import os as _os4
            if _os4.environ.get('NOW2'):
                d = S.op('V', lambda e, c0_=c0, c1_=c1: e.memset(
                    w2s[:, c0_:c1_, :], 0.0), writes=(bw2,))
            else:
                d = S.op('GD', lambda e, c0_=c0, c1_=c1: e.dma_start(
                    out=w2s[:, c0_:c1_, :],
                    in_=w2x_d[l].ap()[c0_:c1_, :, :].rearrange("c p d -> p c d")),
                    writes=(bw2,))
            w2deps.append(d)
        ld('Y', b1_s[:], b1p_d[l].ap(), bB1)
        ld('Y', lngb_s[:], lng_d[l].ap().partition_broadcast(128), B('lngb'))
        ld('Y', lnbb_s[:], lnb_d[l].ap().partition_broadcast(128), B('lnbb'))
        S.op('V', lambda e: e.memset(u21[0:43, :], 1.0), writes=(bu21,))

        NG = NT // 2

        def gload(g):
            i = g % 3
            dz = [B(f"za@{2 * g}").w, B(f"za@{2 * g + 1}").w]
            S.op('Y', lambda e, g_=g: e.dma_start(
                out=zag[i][:].rearrange("p (a d) -> p a d", a=2),
                in_=za_d.ap()[g_ * 256:(g_ + 1) * 256, :].rearrange(
                    "(a p) d -> p a d", p=128)), deps=dz, writes=(bzag[i],))
            dsh = [B(f"zash@{2 * g - 1}").w, B(f"zash@{2 * g}").w,
                   B(f"zash@{2 * g + 1}").w]
            S.op('Y', lambda e, g_=g: e.dma_start(
                out=zsh[i][:].rearrange("p (a d) -> p a d", a=2),
                in_=zash_d.ap()[g_ * 256:(g_ + 1) * 256, :].rearrange(
                    "(a p) d -> p a d", p=128)), deps=dsh, writes=(bzsh[i],))

        def emit_group(g):
            zb, sh = zag[g % 3], zsh[g % 3]
            bzb, bsh = bzag[g % 3], bzsh[g % 3]

            for a in range(2):
                tt = 2 * g + a
                S.op('V', lambda e, a_=a, tt_=tt: e.tensor_scalar(
                    out=lt[:, 0:DI], in0=zb[:, a_ * D:a_ * D + DI],
                    scalar1=ln2_s[:, tt_, 0:1], scalar2=ln2_s[:, tt_, 1:2],
                    op0=ALU.mult, op1=ALU.add), reads=(bzb, bLn2), writes=(blt,))
                S.op('V', lambda e, a_=a, tt_=tt: e.tensor_scalar(
                    out=lt[:, DI:D], in0=sh[:, a_ * DI:(a_ + 1) * DI],
                    scalar1=ln2_s[:, tt_, 0:1], scalar2=ln2_s[:, tt_, 1:2],
                    op0=ALU.mult, op1=ALU.add), reads=(bsh, bLn2), writes=(blt,))
                S.op('V', lambda e: e.tensor_mul(lt[:], lt[:], lngb_s[:]),
                     reads=(B('lngb'), blt), writes=(blt,))
                S.op('V', lambda e, a_=a: e.tensor_add(
                    h2[:, a_ * D:(a_ + 1) * D], lt[:], lnbb_s[:]),
                    reads=(B('lnbb'), blt), writes=(bh2,))

            if debug and l == 0:
                if g == 0:
                    S.op('SD', lambda e: e.dma_start(out=tapLN2_d.ap(), in_=ln2_s[:]),
                         deps=[bLn2.w])
                S.op('SD', lambda e, g_=g: e.dma_start(
                    out=tapH2_d.ap()[g_ * 256:(g_ + 1) * 256, :].rearrange(
                        "(a p) d -> p a d", p=128),
                    in_=h2[:].rearrange("p (a d) -> p a d", a=2)), reads=(bh2,))
            def tr_h2(e):
                for a in range(2):
                    for c in range(8):
                        ins = e.transpose(
                            pT2[:, (a * 8 + c) * TL:(a * 8 + c + 1) * TL],
                            h2[:, a * D + c * TL:a * D + (c + 1) * TL], idb_s[:])
                return ins
            S.op('T', tr_h2, deps=[B('idb').w], reads=(bh2,), writes=(bpT2,))
            S.op('V', lambda e: e.tensor_copy(
                h2fm[:, :, 0:128], pT2[:, 0:1024].rearrange("p (a b) -> p a b", a=8)),
                reads=(bpT2,), writes=(bh2fm,))
            S.op('V', lambda e: e.tensor_copy(
                h2fm[:, :, 128:256],
                pT2[:, 1024:2048].rearrange("p (a b) -> p a b", a=8)),
                reads=(bpT2,), writes=(bh2fm,))

            znb, bznb = zn[g % 2], bzn[g % 2]
            if debug and l == 0:
                S.op('SD', lambda e, g_=g: e.dma_start(
                    out=tapH2F_d.ap()[g_ * 128:(g_ + 1) * 128, :, :], in_=h2fm[:]),
                    reads=(bh2fm,))
            def mm_w1(e, c):
                w, off = _pair_geom(c)
                pag = pAG[c % 2]
                pt = partof[c]
                wsp = w1sp[pt]
                off = off - PLO[pt][0]
                for k in range(8):
                    e.matmul(pag[0:w, 0:256], wsp[:, k, off:off + w],
                             h2fm[:, k, :], start=(k == 0), stop=False)
                for k in range(8):
                    ins = e.matmul(pag[0:w, 256:512],
                                   wsp[:, k, off + w:off + 2 * w],
                                   h2fm[:, k, :], start=(k == 0), stop=(k == 7))
                return ins

            def mm_w2(e, c):
                w, _ = _pair_geom(c)
                wk = w if c < 21 else 43
                u_ = ub[c % 2] if c < 21 else u21
                for a in range(2):
                    pz = pZ[a]
                    for hf in range(2):
                        ins = e.matmul(pz[:, hf * DI:(hf + 1) * DI],
                                       u_[0:wk, a * TL:(a + 1) * TL],
                                       w2s[0:wk, c, hf * DI:(hf + 1) * DI],
                                       start=(c == 0), stop=(c == NPAIR - 1))
                return ins
            S.op('T', lambda e: mm_w1(e, 0), deps=[w1deps[0]],
                 reads=(bh2fm,), writes=(bpAG[0],))
            for c in range(NPAIR):
                w, off = _pair_geom(c)
                pag, bpag = pAG[c % 2], bpAG[c % 2]
                uu = ub[c % 2] if c < 21 else u21
                buu = bub[c % 2] if c < 21 else bu21
                S.op('S', lambda e, w_=w, c_=c, pg=pag: e.activation(
                    gg[0:w_, :], pg[0:w_, 256:512], AF.Gelu,
                    bias=b1_s[0:w_, 2 * c_ + 1:2 * c_ + 2]),
                    reads=(bpag, bB1), writes=(bgg,))
                if c + 1 < NPAIR:
                    S.op('T', lambda e, c_=c + 1: mm_w1(e, c_),
                         deps=[w1deps[partof[c + 1]]],
                         reads=(bh2fm,), writes=(bpAG[(c + 1) % 2],))
                S.op('V', lambda e, w_=w, c_=c, pg=pag, u_=uu: e.scalar_tensor_tensor(
                    out=u_[0:w_, :], in0=pg[0:w_, 0:256],
                    scalar=b1_s[0:w_, 2 * c_:2 * c_ + 1], in1=gg[0:w_, :],
                    op0=ALU.add, op1=ALU.mult),
                    reads=(bpag, bgg, bB1), writes=(buu,))
                if debug and l == 0 and c == 2:
                    S.op('SD', lambda e, g_=g, u_=uu: e.dma_start(
                        out=tapU_d.ap()[g_ * 128:(g_ + 1) * 128, :], in_=u_[0:128, :]),
                        reads=(buu,))
                S.op('T', lambda e, c_=c: mm_w2(e, c_),
                     deps=[w2deps[partof[c]]],
                     reads=(buu,), writes=(bpZ[0], bpZ[1]))

            for a in range(2):
                S.op('V', lambda e, a_=a: e.tensor_add(
                    znb[:, a_ * D:(a_ + 1) * D], pZ[a_][:],
                    zb[:, a_ * D:(a_ + 1) * D]),
                    reads=(bpZ[a], bzb), writes=(bznb,))
            sname2 = 'z2' if last else 'zs'
            S.op('SD', lambda e, g_=g: e.dma_start(
                out=dst_d.ap()[g_ * 256:(g_ + 1) * 256, :].rearrange(
                    "(a p) d -> p a d", p=128),
                in_=znb[:].rearrange("p (a d) -> p a d", a=2)), reads=(bznb,),
                writes=(B(f"{sname2}@{2 * g}"), B(f"{sname2}@{2 * g + 1}")))
            if not last:
                S.op('V', lambda e: e.tensor_copy(znb16[:], znb[:]),
                     reads=(bznb,), writes=(B(f"znb16{tag}"),))
                S.op('SD', lambda e, g_=g: e.dma_start(
                    out=zsh_d.ap()[g_ * 256:(g_ + 1) * 256, 0:DI].rearrange(
                        "(a p) d -> p a d", p=128),
                    in_=znb16[:].rearrange("p (a d) -> p a d", a=2)[:, :, 0:DI]),
                    reads=(B(f"znb16{tag}"),),
                    writes=(B(f"zsh@{2 * g}"), B(f"zsh@{2 * g + 1}")))
                S.op('SD', lambda e, g_=g: e.dma_start(
                    out=zsh_d.ap()[g_ * 256 + 1:(g_ + 1) * 256 + 1, DI:D].rearrange(
                        "(a p) d -> p a d", p=128),
                    in_=znb16[:].rearrange("p (a d) -> p a d", a=2)[:, :, DI:D]),
                    reads=(B(f"znb16{tag}"),),
                    writes=(B(f"zsh@{2 * g}"), B(f"zsh@{2 * g + 1}")))

        gload(0)
        gload(1)
        for g in range(NG):
            if g + 2 < NG:
                gload(g + 2)
            emit_group(g)
        es.close()

    def final_pass():
        S.pass_start()
        es = ExitStack()
        z2t = [sbuf(f"cz{i}", [128, D], F32, es) for i in range(3)]
        x1t = [sbuf(f"cx{i}", [128, D], F32, es) for i in range(3)]
        ot = [sbuf(f"co{i}", [128, D], F32, es) for i in range(2)]
        lt = sbuf("clt", [128, D], F32, es)
        bcz = [B(f"cz{i}") for i in range(3)]
        bcx = [B(f"cx{i}") for i in range(3)]
        bco = [B("co0"), B("co1")]
        bclt = B("clt")

        def cload(t):
            S.op('Y', lambda e, t_=t: e.dma_start(
                out=z2t[t_ % 3][:], in_=z2_d.ap()[t_ * TL:(t_ + 1) * TL, :]),
                deps=[B(f"z2@{t}").w], writes=(bcz[t % 3],))
            S.op('Y', lambda e, t_=t: e.dma_start(
                out=x1t[t_ % 3][:], in_=x1_d.ap()[t_ * TL:(t_ + 1) * TL, :]),
                deps=[B(f"x1@{t}").w], writes=(bcx[t % 3],))
        cload(0)
        cload(1)
        for t in range(NT):
            if t + 2 < NT:
                cload(t + 2)
            zt, bz = z2t[t % 3], bcz[t % 3]

            def lnsrc(e, what, z_=zt):
                if what == 'sum':
                    return e.reduce_sum(stLN[:, 0:1], z_[:], axis=AX.X)
                return e.activation(sq_s[:], z_[:], AF.Square,
                                    accum_out=stLN[:, 1:2])
            ln_stats(lnsrc, (bz,))
            S.op('V', lambda e, z_=zt: e.tensor_scalar(
                out=lt[:], in0=z_[:], scalar1=stLN[:, 4:5], scalar2=stLN[:, 5:6],
                op0=ALU.mult, op1=ALU.add), reads=(bz,), fence_reads=(bLN,),
                writes=(bclt,))
            S.op('V', lambda e, x_=x1t[t % 3], o_=ot[t % 2]: e.tensor_add(
                o_[:], lt[:], x_[:]),
                reads=(bclt, bcx[t % 3]), writes=(bco[t % 2],))
            S.op('SD', lambda e, t_=t, o_=ot[t % 2]: e.dma_start(
                out=out_d.ap()[t_ * TL:(t_ + 1) * TL, :], in_=o_[:]),
                reads=(bco[t % 2],))
        es.close()

    if 'A' in passes:
        attn_pass('A')
    for l in range(2):
        if l in passes or f'B1{l}' in passes:
            attn_pass('B1', l)
        if l in passes or f'F{l}' in passes:
            ffn_pass(l, last=(l == 1))
    if 'C' in passes:
        final_pass()

    emit_es = ExitStack()
    sems = {q: emit_es.enter_context(nc.semaphore(f"sem{q}")) for q in _STEP}
    block = emit_es.enter_context(nc.Block())
    with emit_es:
        def emit_engine(phys):
            def body(h):
                for waits, fn, q in S.thunks[phys]:
                    for (dq, val) in waits:
                        h.wait_ge(sems[dq], val)
                    ins = fn(h)
                    ins.then_inc(sems[q], _STEP[q])
                if phys == 'sync':
                    for q2 in ('SD', 'GD', 'Y'):
                        if S.count[q2] > 0:
                            h.wait_ge(sems[q2], S.count[q2])
            return body
        block.tensor(emit_engine('tensor'))
        block.vector(emit_engine('vector'))
        block.scalar(emit_engine('scalar'))
        block.gpsimd(emit_engine('gpsimd'))
        block.sync(emit_engine('sync'))
    top.close()
    return nc


def kernel(L0_attn_x, L0_attn_Wqkv, L0_attn_Wout, L0_ffn_wqkv, L0_ffn_q_scale,
           L0_ffn_k_scale, L0_ffn_wo, L0_ffn_ln_g, L0_ffn_ln_b, L0_ffn_w1,
           L0_ffn_b1, L0_ffn_w2, L0_ffn_b2):
    C = _host_consts()
    perm = C['perm']
    f32 = np.float32

    x = np.ascontiguousarray(np.asarray(L0_attn_x, dtype=f32))
    shared = {
        "WqkvO": np.ascontiguousarray(np.asarray(L0_attn_Wqkv, f32)),
        "WoutO": np.ascontiguousarray(np.asarray(L0_attn_Wout, f32)),
        "mCAU": C['mCAU'], "mSUPCAU": C['mSUPCAU'], "mUPICAU": C['mUPICAU'],
        "identb": C['identb'], "shiftm": C['shiftm'], "pick127": C['pick127'],
        "oc": C['oc'], "osn": C['os'], "ic": C['ic'], "isn": C['is_'],
        "cvec": np.tile(np.array([[EPS, -0.5, -1.0, D ** -0.5, 1.0 / D, 0.0]], np.float32), (128, 1)),
    }
    for l in range(2):
        shared[f"wqkv{l}"] = np.ascontiguousarray(np.asarray(L0_ffn_wqkv[l], f32))
        shared[f"wo{l}"] = np.ascontiguousarray(np.asarray(L0_ffn_wo[l], f32))
        w1p_full = np.asarray(L0_ffn_w1[l], f32)[:, perm]
        _plo = [(0, 1536), (1536, 2816), (2816, 4096), (4096, 5460)]
        for i, (lo, hi) in enumerate(_plo):
            shared[f"w1q{l}p{i}"] = np.ascontiguousarray(w1p_full[:, lo:hi])
        b1p = np.asarray(L0_ffn_b1[l], f32)[perm]
        b1s = np.zeros((128, 44), f32)
        for c in range(NPAIR):
            w, off = _pair_geom(c)
            b1s[0:w, 2 * c] = b1p[off:off + w]
            b1s[0:w, 2 * c + 1] = b1p[off + w:off + 2 * w]
        shared[f"b1p{l}"] = b1s
        w2 = np.asarray(L0_ffn_w2[l], f32)
        w2x = np.zeros((NPAIR, 128, D), f32)
        for c in range(NPAIR):
            w = 128 if c < 21 else 42
            w2x[c, 0:w, :] = w2[c * 128:c * 128 + w, :]
        w2x[21, 42, :] = np.asarray(L0_ffn_b2[l], f32)
        shared[f"w2x{l}"] = np.ascontiguousarray(w2x)
        shared[f"qsT{l}"] = np.ascontiguousarray(
            np.tile(np.asarray(L0_ffn_q_scale[l], f32), H)[None, :])
        shared[f"ksT{l}"] = np.ascontiguousarray(
            np.tile(np.asarray(L0_ffn_k_scale[l], f32), H)[None, :])
        shared[f"lng{l}"] = np.ascontiguousarray(np.asarray(L0_ffn_ln_g[l], f32)[None, :])
        shared[f"lnb{l}"] = np.ascontiguousarray(np.asarray(L0_ffn_ln_b[l], f32)[None, :])

    nc = build()
    in_maps = [dict(shared, x0=np.ascontiguousarray(x[b])) for b in range(NCORES)]
    res = run_bass_kernel_spmd(nc, in_maps, core_ids=list(range(NCORES)))
    out = np.stack([np.asarray(res.results[b]["out"]) for b in range(NCORES)], axis=0)
    return out.astype(np.float32)


# version 22 (final)
# speedup vs baseline: 1.0643x; 1.0347x over previous
import numpy as np
import ml_dtypes
import concourse.bass as bass
import concourse.mybir as mybir
from concourse.bass_utils import run_bass_kernel_spmd
from contextlib import ExitStack

F32 = mybir.dt.float32
BF16 = mybir.dt.bfloat16
AF = mybir.ActivationFunctionType
ALU = mybir.AluOpType
AX = mybir.AxisListType

NCORES = 8
T = 4096
D = 1024
TL = 128
NT = T // TL
H = 8
HD = 64
DI = 512
HID = 2730
H2 = 2 * HID
NPAIR = 22
EPS = 1e-5
NEG = -1.0e30

_CONST_CACHE = {}


def _host_consts():
    if _CONST_CACHE:
        return _CONST_CACHE
    il = np.arange(TL)[:, None]
    jl = np.arange(TL)[None, :]
    mCAU = np.where(jl <= il, 0.0, NEG).astype(np.float32)
    mSUP = np.where(jl > il, 0.0, NEG).astype(np.float32)
    mUPI = np.where(jl >= il, 0.0, NEG).astype(np.float32)
    mSUPCAU = np.concatenate([mSUP, mCAU], axis=1)
    mUPICAU = np.concatenate([mUPI, mCAU], axis=1)

    identb = np.eye(TL, dtype=np.float32).astype(ml_dtypes.bfloat16)
    shiftm = np.zeros((TL, TL), np.float32)
    for k in range(TL - 1):
        shiftm[k, k + 1] = 1.0
    pick127 = np.zeros((TL, TL), np.float32)
    pick127[127, 0] = 1.0

    t = np.arange(T, dtype=np.float64)
    half = np.arange(0, HD, 2, dtype=np.float64)

    invf = 1.0 / (10000.0 ** (half / HD))
    fr = t[:, None] * invf[None, :]
    cosf = np.concatenate([np.cos(fr), np.cos(fr)], 1)
    sinf = np.concatenate([np.sin(fr), np.sin(fr)], 1)
    s_o = 1.0 / np.sqrt(HD)
    oc = np.concatenate([np.tile(cosf * s_o, (1, H)), np.tile(cosf, (1, H))], 1)
    os_ = np.concatenate([np.tile(sinf * s_o, (1, H)), np.tile(sinf, (1, H))], 1)

    base = (half + 0.4 * HD) / (1.4 * HD)
    power = (t - T // 2) / 256.0
    scale = base[None, :] ** power[:, None]
    scale = np.concatenate([scale, scale], 1)
    ic = np.concatenate([np.tile(cosf * scale * 8.0, (1, H)),
                         np.tile(cosf / scale, (1, H))], 1)
    is_ = np.concatenate([np.tile(sinf * scale * 8.0, (1, H)),
                          np.tile(sinf / scale, (1, H))], 1)

    perm = []
    for c in range(NPAIR):
        w = 128 if c < 21 else 42
        perm.extend(range(c * 128, c * 128 + w))
        perm.extend(range(HID + c * 128, HID + c * 128 + w))
    perm = np.asarray(perm, dtype=np.int64)
    assert perm.shape[0] == H2

    _CONST_CACHE.update(dict(
        mCAU=mCAU, mSUPCAU=mSUPCAU, mUPICAU=mUPICAU, identb=identb,
        shiftm=shiftm, pick127=pick127,
        oc=oc.astype(np.float32), os=os_.astype(np.float32),
        ic=ic.astype(np.float32), is_=is_.astype(np.float32), perm=perm))
    return _CONST_CACHE


def _pair_geom(c):
    w = 128 if c < 21 else 42
    return w, c * 256


_PHYS = {'T': 'tensor', 'V': 'vector', 'S': 'scalar', 'SD': 'scalar',
         'G': 'gpsimd', 'GD': 'gpsimd', 'Y': 'sync'}
_STEP = {'T': 1, 'V': 1, 'S': 1, 'G': 1, 'SD': 16, 'GD': 16, 'Y': 16}
_ENGS = ('tensor', 'vector', 'scalar', 'gpsimd', 'sync')


class Buf:
    __slots__ = ('w', 'r')

    def __init__(self):
        self.w = None
        self.r = []


class Sched:
    def __init__(self):
        self.thunks = {p: [] for p in _ENGS}
        self.count = dict.fromkeys(_STEP, 0)
        self.waited = {}
        self.barrier = []

    def pass_start(self):
        self.barrier = [(q, c) for q, c in self.count.items() if c > 0]

    def op(self, q, fn, deps=(), reads=(), writes=(), fence_reads=()):
        phys = _PHYS[q]
        dd = list(deps)
        if q == 'GD' and self.count['GD'] > 0:
            dd.append(('GD!', self.count['GD']))
        for b in reads:
            dd.append(b.w)
        for b in writes:
            dd.append(b.w)
            dd.extend(b.r)
        fd = []
        for b in fence_reads:
            dd.append(b.w)
            if b.w is not None and _PHYS[b.w[0]] == phys:
                fd.append(b.w)
        if _STEP[q] == 1:
            for b in reads:
                if (b.w is not None and b.w[0] == q
                        and self.count[q] - b.w[1] <= 2):
                    fd.append(b.w)
        waits = []
        for d in fd:
            dq, val = d
            key = (phys, dq, 'f')
            if self.waited.get(key, -1) >= val:
                continue
            self.waited[key] = val
            waits.append((dq, val))
        for d in dd:
            if d is None:
                continue
            dq, val = d
            if dq == 'GD!':
                key = (phys, 'GD')
                if self.waited.get(key, -1) >= val:
                    continue
                self.waited[key] = val
                waits.append(('GD', val))
                continue
            if _PHYS[dq] == phys and (_STEP[dq] == 1 or dq == q):
                continue
            key = (phys, dq)
            if self.waited.get(key, -1) >= val:
                continue
            self.waited[key] = val
            waits.append((dq, val))
        for b in fence_reads:
            pass
        self.count[q] += _STEP[q]
        self.thunks[phys].append((waits, fn, q))
        dep = (q, self.count[q])
        for b in reads:
            b.r.append(dep)
        for b in fence_reads:
            b.r.append(dep)
        for b in writes:
            b.w = dep
            b.r = []
        return dep


def build(debug=False, passes=('A', 0, 1, 'C'), tapkind='A', eo_scratch=True, tapsel='HQROP'):
    nc = bass.Bass("TRN2", target_bir_lowering=False,
                   dynamic_dma_scratch_size=16384)
    S = Sched()

    x0_d = nc.dram_tensor("x0", [T, D], F32, kind="ExternalInput")
    wqkvO_d = nc.dram_tensor("WqkvO", [D, 3 * DI], F32, kind="ExternalInput")
    woutO_d = nc.dram_tensor("WoutO", [DI, D], F32, kind="ExternalInput")
    wqkvI_d = [nc.dram_tensor(f"wqkv{l}", [D, 3 * DI], F32, kind="ExternalInput") for l in range(2)]
    woI_d = [nc.dram_tensor(f"wo{l}", [DI, D], F32, kind="ExternalInput") for l in range(2)]
    _PW = [1536, 1280, 1280, 1364]
    w1p_d = [[nc.dram_tensor(f"w1q{l}p{i}", [D, _PW[i]], F32, kind="ExternalInput")
              for i in range(4)] for l in range(2)]
    b1p_d = [nc.dram_tensor(f"b1p{l}", [128, 44], F32, kind="ExternalInput") for l in range(2)]
    w2x_d = [nc.dram_tensor(f"w2x{l}", [NPAIR, 128, D], F32, kind="ExternalInput") for l in range(2)]
    qs_d = [nc.dram_tensor(f"qsT{l}", [1, DI], F32, kind="ExternalInput") for l in range(2)]
    ks_d = [nc.dram_tensor(f"ksT{l}", [1, DI], F32, kind="ExternalInput") for l in range(2)]
    lng_d = [nc.dram_tensor(f"lng{l}", [1, D], F32, kind="ExternalInput") for l in range(2)]
    lnb_d = [nc.dram_tensor(f"lnb{l}", [1, D], F32, kind="ExternalInput") for l in range(2)]
    mCAU_d = nc.dram_tensor("mCAU", [TL, TL], F32, kind="ExternalInput")
    mSC_d = nc.dram_tensor("mSUPCAU", [TL, 2 * TL], F32, kind="ExternalInput")
    mUC_d = nc.dram_tensor("mUPICAU", [TL, 2 * TL], F32, kind="ExternalInput")
    idb_d = nc.dram_tensor("identb", [TL, TL], BF16, kind="ExternalInput")
    shm_d = nc.dram_tensor("shiftm", [TL, TL], F32, kind="ExternalInput")
    pk_d = nc.dram_tensor("pick127", [TL, TL], F32, kind="ExternalInput")
    cv_d = nc.dram_tensor("cvec", [128, 6], F32, kind="ExternalInput")
    oc_d = nc.dram_tensor("oc", [T, D], F32, kind="ExternalInput")
    os_d = nc.dram_tensor("osn", [T, D], F32, kind="ExternalInput")
    ic_d = nc.dram_tensor("ic", [T, D], F32, kind="ExternalInput")
    is_d = nc.dram_tensor("isn", [T, D], F32, kind="ExternalInput")

    out_d = nc.dram_tensor("out", [T, D], F32, kind="ExternalOutput")
    dbg = "ExternalOutput" if (debug or eo_scratch) else "Internal"
    x1_d = nc.dram_tensor("x1sc", [T, D], F32, kind=dbg)
    z_d = nc.dram_tensor("zsc", [T, D], F32, kind=dbg)
    za_d = nc.dram_tensor("zasc", [T, D], F32, kind=dbg)
    zsh_d = nc.dram_tensor("zshsc", [T + TL, D], BF16, kind="Internal")
    zash_d = nc.dram_tensor("zashsc", [T + TL, DI], BF16, kind="Internal")
    z2_d = nc.dram_tensor("z2sc", [T, D], F32, kind=dbg)
    if debug:
        tapH_d = nc.dram_tensor("tapH", [T, D], BF16, kind="ExternalOutput")
        tapZ_d = nc.dram_tensor("tapZ", [T, D], F32, kind="ExternalOutput")
        tapRot_d = nc.dram_tensor("tapRot", [T, D], F32, kind="ExternalOutput")
        tapP_d = nc.dram_tensor("tapP", [T, 640], BF16, kind="ExternalOutput")
        tapPT_d = nc.dram_tensor("tapPT", [T, 640], BF16, kind="ExternalOutput")
        tapSM_d = nc.dram_tensor("tapSM", [T, 2], F32, kind="ExternalOutput")
        tapV_d = nc.dram_tensor("tapV", [T, DI], BF16, kind="ExternalOutput")
        tapW_d = nc.dram_tensor("tapW", [128, 8 * 3 * DI], BF16, kind="ExternalOutput")
        tapL2_d = nc.dram_tensor("tapL2", [T, 16], F32, kind="ExternalOutput")
        tapL3_d = nc.dram_tensor("tapL3", [T, 16], F32, kind="ExternalOutput")
        tapSQ_d = nc.dram_tensor("tapSQ", [T, D], F32, kind="ExternalOutput")
        tapQR_d = nc.dram_tensor("tapQR", [T, D], F32, kind="ExternalOutput")
        tapLN2_d = nc.dram_tensor("tapLN2", [128, NT, 2], F32, kind="ExternalOutput")
        tapH2_d = nc.dram_tensor("tapH2", [T, D], BF16, kind="ExternalOutput")
        tapH2F_d = nc.dram_tensor("tapH2F", [T // 2, 8, 256], BF16, kind="ExternalOutput")
        tapPA_d = nc.dram_tensor("tapPA", [T, 512], F32, kind="ExternalOutput")
        tapU_d = nc.dram_tensor("tapU", [T, 256], BF16, kind="ExternalOutput")
        tapT2_d = nc.dram_tensor("tapT2", [T, D], F32, kind="ExternalOutput")
        tapT3_d = nc.dram_tensor("tapT3", [T, D], F32, kind="ExternalOutput")
        tapS_d = nc.dram_tensor("tapS", [T, 8], F32, kind="ExternalOutput")
        tapQK_d = nc.dram_tensor("tapQK", [T, 2 * DI], F32, kind="ExternalOutput")
        tapR_d = nc.dram_tensor("tapR", [T, 2 * DI], BF16, kind="ExternalOutput")
        tapO_d = nc.dram_tensor("tapO", [T, DI], BF16, kind="ExternalOutput")
        tapF_d = nc.dram_tensor("tapF", [T, D], BF16, kind="ExternalOutput")

    top = ExitStack()

    def sbuf(name, shape, dt=F32, es=None):
        return (es or top).enter_context(nc.sbuf_tensor(name, shape, dt))

    def psum(name, shape, dt=F32, es=None):
        return (es or top).enter_context(nc.psum_tensor(name, shape, dt))

    mCAU_s = sbuf("mCAUs", [TL, TL])
    mSC_s = sbuf("mSC", [TL, 2 * TL])
    mUC_s = sbuf("mUC", [TL, 2 * TL])
    idb_s = sbuf("idbs", [TL, TL], BF16)
    shm_s = sbuf("shms", [TL, TL])
    pk_s = sbuf("pks", [TL, TL])
    qsb_s = sbuf("qsb", [128, 2 * DI])
    lngb_s = sbuf("lngb", [128, D])
    lnbb_s = sbuf("lnbb", [128, D])
    ln2_s = sbuf("ln2", [128, NT, 2])
    sq_s = sbuf("sqscr", [128, D])
    b1_s = sbuf("b1s", [128, 44])
    cv_s = sbuf("cvs", [128, 6])
    zrow_s = sbuf("zrow", [1, D], BF16)
    stLN = sbuf("stLN", [128, 8])
    stSM = sbuf("stSM", [128, 2])
    stL2 = sbuf("stL2", [128, 16])
    stL2b = sbuf("stL2b", [128, 16])
    stSH = sbuf("stSH", [128, 6])
    stSHp = sbuf("stSHp", [128, 2])

    bufs = {}

    def B(name):
        if name not in bufs:
            b = Buf()
            b.r = list(S.barrier)
            bufs[name] = b
        return bufs[name]

    bLN, bSM, bL2, bSH, bSHp = B('stLN'), B('stSM'), B('stL2'), B('stSH'), B('stSHp')
    bSQ, bLn2, bB1 = B('sqscr'), B('ln2'), B('b1')

    def ld(q, dst_ap, src_ap, wbuf, deps=()):
        return S.op(q, lambda e, d_=dst_ap, s_=src_ap: e.dma_start(out=d_, in_=s_),
                    deps=deps, writes=(wbuf,))

    ld('Y', cv_s[:], cv_d.ap(), B('cv'))
    S.op('V', lambda e: e.memset(sq_s[0:1, 0:DI], 0.0), writes=(B('sqz'),))
    S.op('V', lambda e: e.memset(zrow_s[:], 0.0), writes=(B('zrow'),))
    S.op('SD', lambda e: e.dma_start(out=zsh_d.ap()[0:1, DI:D], in_=zrow_s[0:1, 0:DI]),
         reads=(B('zrow'),), writes=(B('zsh@-1'),))
    S.op('SD', lambda e: e.dma_start(out=zash_d.ap()[0:1, :], in_=zrow_s[0:1, 0:DI]),
         reads=(B('zrow'),), writes=(B('zash@-1'),))
    ld('Y', mCAU_s[:], mCAU_d.ap(), B('mCAU'))
    ld('Y', mSC_s[:], mSC_d.ap(), B('mSC'))
    ld('Y', mUC_s[:], mUC_d.ap(), B('mUC'))
    ld('Y', idb_s[:], idb_d.ap(), B('idb'))
    ld('Y', shm_s[:], shm_d.ap(), B('shm'))
    ld('Y', pk_s[:], pk_d.ap(), B('pk'))

    def ln_stats(srcfn, src_bufs):
        S.op('V', lambda e: srcfn(e, 'sum'), reads=src_bufs, writes=(bLN,))
        S.op('S', lambda e: srcfn(e, 'sq'), reads=src_bufs, writes=(bSQ, bLN))
        S.op('V', lambda e: e.tensor_scalar(
            out=stLN[:, 2:3], in0=stLN[:, 0:1], scalar1=stLN[:, 0:1],
            scalar2=1.0 / D, op0=ALU.mult, op1=ALU.mult),
            fence_reads=(bLN,), writes=(bLN,))
        S.op('V', lambda e: e.tensor_scalar(
            out=stLN[:, 3:4], in0=stLN[:, 1:2], scalar1=stLN[:, 2:3],
            scalar2=1.0 / D, op0=ALU.subtract, op1=ALU.mult),
            fence_reads=(bLN,), writes=(bLN,))
        S.op('S', lambda e: e.activation(stLN[:, 3:4], stLN[:, 3:4], AF.Ln, bias=cv_s[:, 0:1]),
             reads=(bLN, B('cv')), writes=(bLN,))
        S.op('S', lambda e: e.activation(stLN[:, 4:5], stLN[:, 3:4], AF.Exp, scale=cv_s[:, 1:2]),
             reads=(bLN, B('cv')), writes=(bLN,))
        S.op('V', lambda e: e.tensor_scalar(
            out=stLN[:, 5:6], in0=stLN[:, 0:1], scalar1=stLN[:, 4:5],
            scalar2=-1.0 / D, op0=ALU.mult, op1=ALU.mult),
            fence_reads=(bLN,), writes=(bLN,))

    def attn_pass(kind, l=None):
        S.pass_start()
        es = ExitStack()
        inner = kind == 'B1'
        WIN = 4 if inner else 2
        tag = f"{kind}{l}"
        wqkv_s = sbuf(f"wqkv{tag}", [128, 8, 3 * DI], BF16, es)
        wo_s = sbuf(f"wo{tag}", [128, 4, D], BF16, es)
        kfm = sbuf(f"kfm{tag}", [128, 4, T], BF16, es)
        vst = sbuf(f"vst{tag}", [128, NT, H, HD], BF16, es)
        ztile = [sbuf(f"zt{tag}{i}", [128, D], F32, es) for i in range(3)]
        hbuf = [sbuf(f"hb{tag}{i}", [128, D], BF16, es) for i in range(3)]
        ctab = [sbuf(f"ct{tag}{i}", [128, D], F32, es) for i in range(3)]
        stab = [sbuf(f"sn{tag}{i}", [128, D], F32, es) for i in range(3)]
        hfm = sbuf(f"hfm{tag}", [128, 8, TL], BF16, es)
        qk = sbuf(f"qk{tag}", [128, D], F32, es)
        rot = sbuf(f"rot{tag}", [128, D], F32, es)
        qkr = sbuf(f"qkr{tag}", [128, D], BF16, es)
        qfm = [sbuf(f"qfm{tag}{i}", [128, 4, TL], BF16, es) for i in range(2)]
        Pb = [sbuf(f"P{tag}{i}", [128, (WIN + 1) * TL], BF16, es) for i in range(2)]
        Pn = [sbuf(f"Pn{tag}{i}", [128, (WIN + 1) * TL], BF16, es) for i in range(2)]
        stSM2 = [sbuf(f"sm{tag}{i}", [128, 2], F32, es) for i in range(2)]
        PTs = [sbuf(f"PT{tag}{i}", [128, (WIN + 1) * TL], BF16, es) for i in range(2)]
        ofm = [sbuf(f"ofm{tag}{i}", [128, 4, TL], BF16, es) for i in range(2)]
        zares = [sbuf(f"za{tag}{i}", [128, D], F32, es) for i in range(2)]
        lnt = sbuf(f"lnt{tag}", [128, D], F32, es)
        lnb = sbuf(f"lnbf{tag}", [128, D], BF16, es)
        sqP = sbuf(f"sqP{tag}", [128, D], F32, es)
        sqE = sbuf(f"sqE{tag}", [128, D], F32, es)
        stP = sbuf(f"stP{tag}", [128, 8], F32, es)
        stE = sbuf(f"stE{tag}", [128, 8], F32, es)
        stZ = sbuf(f"stZ{tag}", [128, 8], F32, es)
        stL2a = sbuf(f"stL2a{tag}", [128, 16], F32, es)
        stL2b = sbuf(f"stL2b{tag}", [128, 16], F32, es)
        pP1 = psum(f"pP1{tag}", [128, 1024], F32, es)
        pPtr = psum(f"pPtr{tag}", [128, 1024], BF16, es)
        pA0 = psum(f"pA0{tag}", [128, 1024], F32, es)
        pA1 = psum(f"pA1{tag}", [128, 1024], F32, es)
        pO = psum(f"pO{tag}", [128, 128], F32, es)
        pPT0 = pPtr
        pA = [pA0, pA1]
        bpP1 = B(f"pP1{tag}")
        bpPtr = B(f"pPtr{tag}")
        bpA = [B(f"pA0{tag}"), B(f"pA1{tag}")]
        bpPT0, bpO = bpPtr, B(f"pO{tag}")
        bkfm, bvst = B(f"kfm{tag}"), B(f"vst{tag}")
        bwqkv, bwo = B(f"wqkv{tag}"), B(f"wo{tag}")
        bhfm, bqk = B(f"hfm{tag}"), B(f"qk{tag}")
        brot = B(f"rot{tag}")
        bqkr = B(f"qkr{tag}")
        bqfm = [B(f"qfm{tag}0"), B(f"qfm{tag}1")]
        bP = [B(f"P{tag}0"), B(f"P{tag}1")]
        bPn = [B(f"Pn{tag}0"), B(f"Pn{tag}1")]
        bSM2 = [B(f"sm{tag}0"), B(f"sm{tag}1")]
        bPT = [B(f"PT{tag}0"), B(f"PT{tag}1")]
        bofm = [B(f"ofm{tag}0"), B(f"ofm{tag}1")]
        blnt, blnb = B(f"lnt{tag}"), B(f"lnbf{tag}")
        bsqP, bsqE = B(f"sqP{tag}"), B(f"sqE{tag}")
        bstP, bstE, bstZ = B(f"stP{tag}"), B(f"stE{tag}"), B(f"stZ{tag}")
        bL2a, bL2b = B(f"stL2a{tag}"), B(f"stL2b{tag}")
        bzt = [B(f"zt{tag}{i}") for i in range(3)]
        bhb = [B(f"hb{tag}{i}") for i in range(3)]
        bct = [B(f"ct{tag}{i}") for i in range(3)]
        bsn = [B(f"sn{tag}{i}") for i in range(3)]
        bza = [B(f"za{tag}{i}") for i in range(2)]

        src_d = x0_d if kind == 'A' else z_d
        ct_d = oc_d if kind == 'A' else ic_d
        st_d = os_d if kind == 'A' else is_d

        wq_src = wqkvO_d if kind == 'A' else wqkvI_d[l]
        wo_src = woutO_d if kind == 'A' else woI_d[l]
        ld('GD', wqkv_s[:], wq_src.ap().rearrange("(k p) n -> p k n", p=128), bwqkv)
        ld('GD', wo_s[:], wo_src.ap().rearrange("(k p) n -> p k n", p=128), bwo)
        if inner:
            ld('Y', qsb_s[:, 0:DI], qs_d[l].ap().partition_broadcast(128), B('qsb'))
            ld('Y', qsb_s[:, DI:2 * DI], ks_d[l].ap().partition_broadcast(128), B('qsb'))

        def ln_chain(st, bst, sumfn, sqfn, sb, bsb, extradep=()):
            S.op('V', sumfn, deps=list(extradep), writes=(bst,))
            S.op('S', sqfn, deps=list(extradep), writes=(bsb, bst))
            S.op('V', lambda e: e.tensor_scalar(
                out=st[:, 2:3], in0=st[:, 0:1], scalar1=st[:, 0:1],
                scalar2=1.0 / (D * D), op0=ALU.mult, op1=ALU.mult),
                fence_reads=(bst,), writes=(bst,))
            S.op('V', lambda e: e.tensor_scalar(
                out=st[:, 3:4], in0=st[:, 1:2], scalar1=st[:, 2:3],
                scalar2=1.0, op0=ALU.subtract, op1=ALU.mult),
                fence_reads=(bst,), writes=(bst,))
            S.op('S', lambda e: e.activation(st[:, 3:4], st[:, 3:4], AF.Ln,
                                             bias=cv_s[:, 0:1]),
                 reads=(bst, B('cv')), writes=(bst,))
            S.op('S', lambda e: e.activation(st[:, 4:5], st[:, 3:4], AF.Exp,
                                             scale=cv_s[:, 1:2]),
                 fence_reads=(bst,), writes=(bst,))
            S.op('V', lambda e: e.tensor_scalar(
                out=st[:, 5:6], in0=st[:, 0:1], scalar1=st[:, 4:5],
                scalar2=-1.0 / D, op0=ALU.mult, op1=ALU.mult),
                fence_reads=(bst,), writes=(bst,))

        def tload(t):
            i4 = t % 3
            S.op('Y', lambda e, t_=t: e.dma_start(
                out=ztile[i4][:], in_=src_d.ap()[t_ * TL:(t_ + 1) * TL, :]),
                deps=[B(f"{sname}@{t}").w], writes=(bzt[i4],))
            ld('Y', ctab[i4][:], ct_d.ap()[t * TL:(t + 1) * TL, :], bct[i4])
            ld('Y', stab[i4][:], st_d.ap()[t * TL:(t + 1) * TL, :], bsn[i4])
            if inner:
                S.op('Y', lambda e, t_=t, hb=hbuf[i4]: e.dma_start(
                    out=hb[:], in_=zsh_d.ap()[t_ * TL:(t_ + 1) * TL, :]),
                    deps=[B(f"zsh@{t - 1}").w, B(f"zsh@{t}").w], writes=(bhb[i4],))

        sname = 'x0' if kind == 'A' else 'zs'

        def stage_p1(t):
            if kind != 'A':
                return
            zt, bz = ztile[t % 3], bzt[t % 3]
            h, bh = hbuf[t % 3], bhb[t % 3]

            def sumfn(e, z_=zt):
                return e.reduce_sum(stP[:, 0:1], z_[:], axis=AX.X)

            def sqfn(e, z_=zt):
                return e.activation(sqP[:], z_[:], AF.Square,
                                    scale=cv_s[:, 3:4], accum_out=stP[:, 1:2])
            ln_chain(stP, bstP, sumfn, sqfn, sqP, bsqP, extradep=[bz.w])
            S.op('V', lambda e, z_=zt, h_=h: e.tensor_scalar(
                out=h_[:], in0=z_[:], scalar1=stP[:, 4:5], scalar2=stP[:, 5:6],
                op0=ALU.mult, op1=ALU.add), reads=(bz,), fence_reads=(bstP,),
                writes=(bh,))

        def stage_p(t):
            zt, bz = ztile[t % 3], bzt[t % 3]
            cb, bcb = ctab[t % 3], bct[t % 3]
            sn, bsb = stab[t % 3], bsn[t % 3]
            h, bh = hbuf[t % 3], bhb[t % 3]

            if debug and kind == tapkind and 'H' in tapsel:
                S.op('SD', lambda e, t_=t, h_=h: e.dma_start(
                    out=tapH_d.ap()[t_ * TL:(t_ + 1) * TL, :], in_=h_[:]), reads=(bh,))
            def tr_h(e, h_=h):
                for c in range(8):
                    ins = e.transpose(pPtr[:, c * TL:(c + 1) * TL],
                                      h_[:, c * TL:(c + 1) * TL], idb_s[:])
                return ins
            S.op('T', tr_h, deps=[B('idb').w], reads=(bh,), writes=(bpPtr,))
            S.op('V', lambda e: e.tensor_copy(
                hfm[:].rearrange("p a b -> p (a b)"), pPtr[:]),
                reads=(bpPtr,), writes=(bhfm,))

            def mm_qkv(e, c0, c1, dst, o0, o1):
                for k in range(8):
                    ins = e.matmul(dst[:, o0:o1], hfm[:, k, :],
                                   wqkv_s[:, k, c0:c1], start=(k == 0), stop=(k == 7))
                return ins
            S.op('T', lambda e: mm_qkv(e, 0, DI, pP1, 0, DI),
                 reads=(bhfm, bwqkv), writes=(bpP1,))
            S.op('T', lambda e: mm_qkv(e, DI, 2 * DI, pP1, DI, 2 * DI),
                 reads=(bhfm, bwqkv), writes=(bpP1,))
            S.op('V', lambda e: e.tensor_copy(qk[:], pP1[:]),
                 reads=(bpP1,), writes=(bqk,))
            S.op('T', lambda e: mm_qkv(e, 2 * DI, 3 * DI, pP1, 0, DI),
                 reads=(bhfm, bwqkv), writes=(bpP1,))
            S.op('S', lambda e, t_=t: e.copy(
                vst[:, t_, :, :].rearrange("p a b -> p (a b)"), pP1[:, 0:DI]),
                reads=(bpP1,), writes=(bvst,))

            if inner:
                S.op('V', lambda e: e.tensor_mul(sqP[:], qk[:], qk[:]),
                     reads=(bqk,), writes=(bsqP,))
                S.op('V', lambda e: e.reduce_sum(
                    stL2a[:], sqP[:].rearrange("p (s c) -> p s c", s=16), axis=AX.X),
                    fence_reads=(bsqP,), writes=(bL2a,))
                S.op('V', lambda e: e.tensor_scalar(
                    out=stL2b[:], in0=stL2a[:], scalar1=1e-24, scalar2=None,
                    op0=ALU.max), fence_reads=(bL2a,), writes=(bL2b,))
                S.op('S', lambda e: e.activation(stL2a[:], stL2b[:], AF.Ln),
                     reads=(bL2b,), writes=(bL2a,))
                S.op('S', lambda e: e.activation(stL2b[:], stL2a[:], AF.Exp,
                                                 scale=cv_s[:, 1:2]),
                     reads=(bL2a,), writes=(bL2b,))

                def apply_rstd(e):
                    for i in range(16):
                        ins = e.tensor_scalar(
                            out=qk[:, i * 64:(i + 1) * 64],
                            in0=qk[:, i * 64:(i + 1) * 64],
                            scalar1=stL2b[:, i:i + 1], scalar2=None, op0=ALU.mult)
                    return ins
                S.op('V', apply_rstd, fence_reads=(bL2b,), writes=(bqk,))
                S.op('V', lambda e: e.tensor_mul(qk[:], qk[:], qsb_s[:]),
                     reads=(B('qsb'), bqk), writes=(bqk,))

            S.op('S', lambda e: e.mul(
                rot[:].rearrange("p (s c) -> p s c", s=16)[:, :, 0:32],
                qk[:].rearrange("p (s c) -> p s c", s=16)[:, :, 32:64], cv_s[:, 2:3]),
                reads=(bqk, B('cv')), writes=(brot,))
            S.op('S', lambda e: e.copy(
                rot[:].rearrange("p (s c) -> p s c", s=16)[:, :, 32:64],
                qk[:].rearrange("p (s c) -> p s c", s=16)[:, :, 0:32]),
                reads=(bqk,), writes=(brot,))
            S.op('V', lambda e, c_=cb: e.tensor_mul(qk[:], qk[:], c_[:]),
                 reads=(bcb,), writes=(bqk,))
            S.op('G', lambda e, s_=sn: e.tensor_mul(rot[:], rot[:], s_[:]),
                 reads=(bsb,), writes=(brot,))
            S.op('V', lambda e: e.tensor_add(qkr[:], qk[:], rot[:]),
                 reads=(bqk, brot), writes=(bqkr,))

            if debug and kind == tapkind and 'Q' in tapsel:
                S.op('SD', lambda e, t_=t: e.dma_start(
                    out=tapQK_d.ap()[t_ * TL:(t_ + 1) * TL, :], in_=qk[:]), reads=(bqk,))
            if debug and kind == tapkind and 'R' in tapsel:
                S.op('SD', lambda e, t_=t: e.dma_start(
                    out=tapR_d.ap()[t_ * TL:(t_ + 1) * TL, :], in_=qkr[:]), reads=(bqkr,))
            def tr_qk(e):
                for c in range(8):
                    ins = e.transpose(pPtr[:, c * TL:(c + 1) * TL],
                                      qkr[:, c * TL:(c + 1) * TL], idb_s[:])
                return ins
            S.op('T', tr_qk, deps=[B('idb').w], reads=(bqkr,), writes=(bpPtr,))
            qf = qfm[t % 2]
            S.op('S', lambda e, q_=qf: e.copy(
                q_[:].rearrange("p a b -> p (a b)"), pPtr[:, 0:DI]),
                reads=(bpPtr,), writes=(bqfm[t % 2],))
            S.op('S', lambda e, t_=t: e.copy(
                kfm[:, :, t_ * TL:(t_ + 1) * TL],
                pPtr[:, DI:2 * DI].rearrange("p (a b) -> p a b", a=4)),
                reads=(bpPtr,), writes=(bkfm,))

        def stage_a(t):
            t0 = max(0, t - WIN)
            nck = t - t0 + 1
            nk = nck * TL
            own = (t - t0) * TL
            qf, bqf = qfm[t % 2], bqfm[t % 2]
            of, bof = ofm[t % 2], bofm[t % 2]

            def mm_scores(e, hh):
                p_, s_ = hh // 2, hh % 2
                for o in range(0, nk, 512):
                    o2 = min(nk, o + 512)
                    ins = e.matmul(pA[hh % 2][:, o:o2],
                                   qf[s_ * 64:(s_ + 1) * 64, p_, :],
                                   kfm[s_ * 64:(s_ + 1) * 64, p_,
                                       t0 * TL + o:t0 * TL + o2],
                                   start=True, stop=True)
                return ins
            S.op('T', lambda e: mm_scores(e, 0), reads=(bqf, bkfm), writes=(bpA[0],))
            S.op('T', lambda e: mm_scores(e, 1), reads=(bqf, bkfm), writes=(bpA[1],))
            for hh in range(H):
                pg = hh % 2
                if t >= WIN:
                    medge = mSC_s if kind == 'A' else mUC_s
                    S.op('V', lambda e, g=pg, m=medge: e.tensor_tensor(
                        out=bass.AP(pA[g], 0, [[1024, 128], [own, 2], [1, TL]]),
                        in0=bass.AP(pA[g], 0, [[1024, 128], [own, 2], [1, TL]]),
                        in1=m[:].rearrange("p (c f) -> p c f", c=2), op=ALU.add),
                        deps=[B('mSC').w, B('mUC').w],
                        reads=(bpA[pg],), writes=(bpA[pg],))
                else:
                    S.op('V', lambda e, g=pg: e.tensor_tensor(
                        out=pA[g][:, own:own + TL], in0=pA[g][:, own:own + TL],
                        in1=mCAU_s[:], op=ALU.add),
                        deps=[B('mCAU').w], reads=(bpA[pg],), writes=(bpA[pg],))
                sm, bsm = stSM2[pg], bSM2[pg]
                pb_, bpb = Pb[pg], bP[pg]
                pn_, bpn = Pn[pg], bPn[pg]
                S.op('S', lambda e, g=pg, s_=sm, p_=pb_: e.activation(
                    p_[:, 0:nk], pA[g][:, 0:nk], AF.Exp,
                    accum_out=s_[:, 1:2]),
                    reads=(bpA[pg],), writes=(bpb, bsm))
                if hh + 2 < H:
                    S.op('T', lambda e, n=hh + 2: mm_scores(e, n),
                         reads=(bqf, bkfm), writes=(bpA[pg],))
                S.op('V', lambda e, s_=sm: e.reciprocal(s_[:, 0:1], s_[:, 1:2]),
                     reads=(bsm,), writes=(bsm,))
                S.op('S', lambda e, s_=sm, p_=pb_, n_=pn_: e.activation(
                    n_[:, 0:nk], p_[:, 0:nk], AF.Copy, scale=s_[:, 0:1]),
                    reads=(bpb, bsm), writes=(bpn,))

                if debug and kind == tapkind and hh == 0 and 'P' in tapsel:
                    S.op('SD', lambda e, t_=t, n_=nk, q_=pn_: e.dma_start(
                        out=tapP_d.ap()[t_ * TL:(t_ + 1) * TL, 0:n_], in_=q_[:, 0:n_]),
                        reads=(bpn,))
                def tr_p(e, q_=pn_):
                    for c in range(nck):
                        ins = e.transpose(pPT0[:, c * TL:(c + 1) * TL],
                                          q_[:, c * TL:(c + 1) * TL], idb_s[:])
                    return ins
                S.op('T', tr_p, reads=(bpn,), writes=(bpPT0,))
                S.op('S', lambda e, g=pg: e.copy(PTs[g][:, 0:nk], pPT0[:, 0:nk]),
                     reads=(bpPT0,), writes=(bPT[pg],))

                def mm_av(e, g=pg, h_=hh):
                    for c in range(nck):
                        ins = e.matmul(pO[0:64, :], vst[:, t0 + c, h_, :],
                                       PTs[g][:, c * TL:(c + 1) * TL],
                                       start=(c == 0), stop=(c == nck - 1))
                    return ins
                S.op('T', mm_av, reads=(bPT[pg], bvst), writes=(bpO,))
                S.op('S', lambda e, o_=of, p_=hh // 2, s_=hh % 2: e.copy(
                    o_[s_ * 64:(s_ + 1) * 64, p_, :], pO[0:64, :]),
                    reads=(bpO,), writes=(bof,))

        def stage_e(t):
            zt, bz = ztile[t % 3], bzt[t % 3]
            of, bof = ofm[t % 2], bofm[t % 2]
            za, bzr = zares[t % 2], bza[t % 2]

            if debug and kind == tapkind and 'O' in tapsel:
                S.op('SD', lambda e, t_=t, o_=of: e.dma_start(
                    out=tapO_d.ap()[t_ * TL:(t_ + 1) * TL, :],
                    in_=o_[:].rearrange("p a b -> p (a b)")), reads=(bof,))
            def mm_wo(e, o0, o1):
                for c in range(4):
                    ins = e.matmul(pA0[:, o0:o1], of[:, c, :], wo_s[:, c, o0:o1],
                                   start=(c == 0), stop=(c == 3))
                return ins
            S.op('T', lambda e: mm_wo(e, 0, DI), reads=(bof, bwo), writes=(bpA[0],))
            S.op('T', lambda e: mm_wo(e, DI, D), reads=(bof, bwo), writes=(bpA[0],))

            if kind == 'A':
                def sumfn(e):
                    return e.reduce_sum(stE[:, 0:1], pA0[:], axis=AX.X)

                def sqfn(e):
                    return e.activation(sqE[:], pA0[:], AF.Square,
                                        scale=cv_s[:, 3:4], accum_out=stE[:, 1:2])
                ln_chain(stE, bstE, sumfn, sqfn, sqE, bsqE, extradep=[bpA[0].w])
                S.op('V', lambda e: e.tensor_scalar(
                    out=lnt[:], in0=pA0[:], scalar1=stE[:, 4:5], scalar2=stE[:, 5:6],
                    op0=ALU.mult, op1=ALU.add), reads=(bpA[0],), fence_reads=(bstE,),
                    writes=(blnt,))
                S.op('V', lambda e, z_=zt: e.tensor_add(za[:], lnt[:], z_[:]),
                     reads=(blnt, bz), writes=(bzr,))
                S.op('SD', lambda e, t_=t: e.dma_start(
                    out=x1_d.ap()[t_ * TL:(t_ + 1) * TL, :], in_=za[:]),
                    reads=(bzr,), writes=(B(f"x1@{t}"),))

                def sumfn2(e):
                    return e.reduce_sum(stZ[:, 0:1], za[:], axis=AX.X)

                def sqfn2(e):
                    return e.activation(sqE[:], za[:], AF.Square,
                                        scale=cv_s[:, 3:4], accum_out=stZ[:, 1:2])
                ln_chain(stZ, bstZ, sumfn2, sqfn2, sqE, bsqE, extradep=[bzr.w])
                S.op('V', lambda e: e.tensor_scalar(
                    out=lnt[:], in0=za[:], scalar1=stZ[:, 4:5], scalar2=stZ[:, 5:6],
                    op0=ALU.mult, op1=ALU.add), reads=(bzr,), fence_reads=(bstZ,),
                    writes=(blnt,))
                S.op('V', lambda e: e.tensor_copy(lnb[:], lnt[:]),
                     reads=(blnt,), writes=(blnb,))
                S.op('SD', lambda e, t_=t: e.dma_start(
                    out=z_d.ap()[t_ * TL:(t_ + 1) * TL, :], in_=lnt[:]),
                    reads=(blnt,), writes=(B(f"zs@{t}"),))
                S.op('SD', lambda e, t_=t: e.dma_start(
                    out=zsh_d.ap()[t_ * TL:(t_ + 1) * TL, 0:DI],
                    in_=lnb[:, 0:DI]), reads=(blnb,), writes=(B(f"zsh@{t}"),))
                S.op('SD', lambda e, t_=t: e.dma_start(
                    out=zsh_d.ap()[t_ * TL + 1:(t_ + 1) * TL + 1, DI:D],
                    in_=lnb[:, DI:D]), reads=(blnb,), writes=(B(f"zsh@{t}"),))
            else:
                S.op('V', lambda e, z_=zt: e.tensor_add(za[:], pA0[:], z_[:]),
                     reads=(bpA[0], bz), writes=(bzr,))
                S.op('SD', lambda e, t_=t: e.dma_start(
                    out=za_d.ap()[t_ * TL:(t_ + 1) * TL, :], in_=za[:]),
                    reads=(bzr,), writes=(B(f"za@{t}"),))
                S.op('V', lambda e: e.tensor_copy(lnb[:, 0:DI], za[:, DI:D]),
                     reads=(bzr,), writes=(blnb,))
                S.op('SD', lambda e, t_=t: e.dma_start(
                    out=zash_d.ap()[t_ * TL + 1:(t_ + 1) * TL + 1, :],
                    in_=lnb[:, 0:DI]), reads=(blnb,), writes=(B(f"zash@{t}"),))
                S.op('V', lambda e: e.reduce_sum(stE[:, 0:1], za[:, 0:DI], axis=AX.X),
                     reads=(bzr,), writes=(bstE,))
                S.op('V', lambda e: e.reduce_sum(stE[:, 2:3], za[:, DI:D], axis=AX.X),
                     reads=(bzr,), writes=(bstE,))
                S.op('S', lambda e: e.activation(sqE[:, 0:DI], za[:, 0:DI], AF.Square,
                                                 accum_out=stE[:, 1:2]),
                     reads=(bzr,), writes=(bsqE, bstE))
                S.op('S', lambda e: e.activation(sqE[:, DI:D], za[:, DI:D], AF.Square,
                                                 accum_out=stE[:, 3:4]),
                     reads=(bzr,), writes=(bsqE, bstE))

                def mm_shift(e, t_=t):
                    ins = e.matmul(pO[:, 0:2], shm_s[:], stE[:, 2:4],
                                   start=True, stop=(t_ == 0))
                    if t_ > 0:
                        ins = e.matmul(pO[:, 0:2], pk_s[:], stSHp[:],
                                       start=False, stop=True)
                    return ins
                S.op('T', mm_shift, deps=[B('shm').w, B('pk').w],
                     reads=(bstE, bSHp), writes=(bpO,))
                S.op('V', lambda e: e.tensor_copy(stSHp[:], stE[:, 2:4]),
                     reads=(bstE,), writes=(bSHp,))
                S.op('V', lambda e: e.tensor_add(stSH[:, 4:6], stE[:, 0:2],
                                                 pO[:, 0:2]),
                     reads=(bstE, bpO), writes=(bSH,))
                S.op('V', lambda e: e.tensor_scalar(
                    out=stSH[:, 0:1], in0=stSH[:, 4:5], scalar1=stSH[:, 4:5],
                    scalar2=1.0 / D, op0=ALU.mult, op1=ALU.mult),
                    fence_reads=(bSH,), writes=(bSH,))
                S.op('V', lambda e: e.tensor_scalar(
                    out=stSH[:, 1:2], in0=stSH[:, 5:6], scalar1=stSH[:, 0:1],
                    scalar2=1.0 / D, op0=ALU.subtract, op1=ALU.mult),
                    fence_reads=(bSH,), writes=(bSH,))
                S.op('S', lambda e: e.activation(stSH[:, 1:2], stSH[:, 1:2], AF.Ln,
                                                 bias=cv_s[:, 0:1]),
                     reads=(bSH, B('cv')), writes=(bSH,))
                S.op('S', lambda e, t_=t: e.activation(
                    ln2_s[:, t_, 0:1], stSH[:, 1:2], AF.Exp, scale=cv_s[:, 1:2]),
                    reads=(bSH,), writes=(bLn2,))
                S.op('V', lambda e, t_=t: e.tensor_scalar(
                    out=ln2_s[:, t_, 1:2], in0=stSH[:, 4:5],
                    scalar1=ln2_s[:, t_, 0:1], scalar2=-1.0 / D,
                    op0=ALU.mult, op1=ALU.mult), reads=(bSH,),
                    fence_reads=(bLn2,), writes=(bLn2,))

        tload(0)
        tload(1)
        tload(2)
        stage_p1(0)
        stage_p1(1)
        stage_p1(2)
        stage_p(0)
        stage_p(1)
        for t in range(NT):
            stage_a(t)
            stage_e(t)
            if t + 3 < NT:
                tload(t + 3)
                stage_p1(t + 3)
            if t + 2 < NT:
                stage_p(t + 2)
        es.close()


    def ffn_pass(l, last):
        S.pass_start()
        es = ExitStack()
        tag = f"F{l}"
        PARTS = [(0, 6), (6, 11), (11, 16), (16, 22)]
        PLO = []
        w1sp = []
        for i, (c0, c1) in enumerate(PARTS):
            lo = _pair_geom(c0)[1]
            wl, ol = _pair_geom(c1 - 1)
            hi = ol + 2 * wl
            PLO.append((lo, hi))
            w1sp.append(sbuf(f"w1s{tag}p{i}", [128, 8, hi - lo], BF16, es))
        w2s = sbuf(f"w2s{tag}", [128, NPAIR, D], BF16, es)
        zag = [sbuf(f"zag{tag}{i}", [128, 2 * D], F32, es) for i in range(3)]
        zsh = [sbuf(f"zsh{tag}{i}", [128, 2 * DI], BF16, es) for i in range(3)]
        h2 = sbuf(f"h2{tag}", [128, 2 * D], BF16, es)
        h2fm = sbuf(f"h2fm{tag}", [128, 8, 256], BF16, es)
        lt = sbuf(f"lt{tag}", [128, D], F32, es)
        gg = sbuf(f"gg{tag}", [128, 256], BF16, es)
        ub = [sbuf(f"ub{tag}{i}", [128, 256], BF16, es) for i in range(2)]
        u21 = sbuf(f"u21{tag}", [43, 256], BF16, es)
        zn = [sbuf(f"zn{tag}0", [128, 2 * D], F32, es)] * 2
        znb16 = None if last else sbuf(f"znb16{tag}", [128, 2 * D], BF16, es)
        pZ0 = psum(f"pZ0{tag}", [128, 1024], F32, es)
        pZ1 = psum(f"pZ1{tag}", [128, 1024], F32, es)
        pA = psum(f"pA{tag}", [128, 512], F32, es)
        pB = psum(f"pB{tag}", [128, 512], F32, es)
        pT2 = psum(f"pT2{tag}", [128, 2048], BF16, es)
        pZ = [pZ0, pZ1]
        pAG = [pA, pB]
        bpZ = [B(f"pZ0{tag}"), B(f"pZ1{tag}")]
        bpAG = [B(f"pA{tag}"), B(f"pB{tag}")]
        bpT2 = B(f"pT2{tag}")
        bzag = [B(f"zag{tag}{i}") for i in range(3)]
        bzsh = [B(f"zsh{tag}{i}") for i in range(3)]
        bh2, bh2fm, blt = B(f"h2{tag}"), B(f"h2fm{tag}"), B(f"lt{tag}")
        bgg = B(f"gg{tag}")
        bub = [B(f"ub{tag}{i}") for i in range(2)]
        bu21 = B(f"u21{tag}")
        bzn = [B(f"zn{tag}0")] * 2
        bw1, bw2 = B(f"w1s{tag}"), B(f"w2s{tag}")

        dst_d = z2_d if last else z_d
        partof = [next(i for i, (c0, c1) in enumerate(PARTS) if c0 <= c < c1)
                  for c in range(NPAIR)]
        w1deps, w2deps = [], []
        for i, (c0, c1) in enumerate(PARTS):
            lo, hi = PLO[i]
            import os as _os5
            if _os5.environ.get('NOW1'):
                d = S.op('V', lambda e, i_=i: e.memset(w1sp[i_][:], 0.0), writes=(bw1,))
            else:
                d = S.op('GD', lambda e, i_=i: e.dma_start(
                    out=w1sp[i_][:],
                    in_=w1p_d[l][i_].ap().rearrange("(k p) n -> p k n", p=128)),
                    writes=(bw1,))
            w1deps.append(d)
            import os as _os4
            if _os4.environ.get('NOW2'):
                d = S.op('V', lambda e, c0_=c0, c1_=c1: e.memset(
                    w2s[:, c0_:c1_, :], 0.0), writes=(bw2,))
            else:
                d = S.op('GD', lambda e, c0_=c0, c1_=c1: e.dma_start(
                    out=w2s[:, c0_:c1_, :],
                    in_=w2x_d[l].ap()[c0_:c1_, :, :].rearrange("c p d -> p c d")),
                    writes=(bw2,))
            w2deps.append(d)
        ld('Y', b1_s[:], b1p_d[l].ap(), bB1)
        ld('Y', lngb_s[:], lng_d[l].ap().partition_broadcast(128), B('lngb'))
        ld('Y', lnbb_s[:], lnb_d[l].ap().partition_broadcast(128), B('lnbb'))
        S.op('V', lambda e: e.memset(u21[0:43, :], 1.0), writes=(bu21,))

        NG = NT // 2

        def gload(g):
            i = g % 3
            dz = [B(f"za@{2 * g}").w, B(f"za@{2 * g + 1}").w]
            S.op('Y', lambda e, g_=g: e.dma_start(
                out=zag[i][:].rearrange("p (a d) -> p a d", a=2),
                in_=za_d.ap()[g_ * 256:(g_ + 1) * 256, :].rearrange(
                    "(a p) d -> p a d", p=128)), deps=dz, writes=(bzag[i],))
            dsh = [B(f"zash@{2 * g - 1}").w, B(f"zash@{2 * g}").w,
                   B(f"zash@{2 * g + 1}").w]
            S.op('Y', lambda e, g_=g: e.dma_start(
                out=zsh[i][:].rearrange("p (a d) -> p a d", a=2),
                in_=zash_d.ap()[g_ * 256:(g_ + 1) * 256, :].rearrange(
                    "(a p) d -> p a d", p=128)), deps=dsh, writes=(bzsh[i],))

        def emit_group(g):
            zb, sh = zag[g % 3], zsh[g % 3]
            bzb, bsh = bzag[g % 3], bzsh[g % 3]

            for a in range(2):
                tt = 2 * g + a
                S.op('V', lambda e, a_=a, tt_=tt: e.tensor_scalar(
                    out=lt[:, 0:DI], in0=zb[:, a_ * D:a_ * D + DI],
                    scalar1=ln2_s[:, tt_, 0:1], scalar2=ln2_s[:, tt_, 1:2],
                    op0=ALU.mult, op1=ALU.add), reads=(bzb, bLn2), writes=(blt,))
                S.op('V', lambda e, a_=a, tt_=tt: e.tensor_scalar(
                    out=lt[:, DI:D], in0=sh[:, a_ * DI:(a_ + 1) * DI],
                    scalar1=ln2_s[:, tt_, 0:1], scalar2=ln2_s[:, tt_, 1:2],
                    op0=ALU.mult, op1=ALU.add), reads=(bsh, bLn2), writes=(blt,))
                S.op('V', lambda e: e.tensor_mul(lt[:], lt[:], lngb_s[:]),
                     reads=(B('lngb'), blt), writes=(blt,))
                S.op('V', lambda e, a_=a: e.tensor_add(
                    h2[:, a_ * D:(a_ + 1) * D], lt[:], lnbb_s[:]),
                    reads=(B('lnbb'), blt), writes=(bh2,))

            if debug and l == 0:
                if g == 0:
                    S.op('SD', lambda e: e.dma_start(out=tapLN2_d.ap(), in_=ln2_s[:]),
                         deps=[bLn2.w])
                S.op('SD', lambda e, g_=g: e.dma_start(
                    out=tapH2_d.ap()[g_ * 256:(g_ + 1) * 256, :].rearrange(
                        "(a p) d -> p a d", p=128),
                    in_=h2[:].rearrange("p (a d) -> p a d", a=2)), reads=(bh2,))
            def tr_h2(e):
                for a in range(2):
                    for c in range(8):
                        ins = e.transpose(
                            pT2[:, (a * 8 + c) * TL:(a * 8 + c + 1) * TL],
                            h2[:, a * D + c * TL:a * D + (c + 1) * TL], idb_s[:])
                return ins
            S.op('T', tr_h2, deps=[B('idb').w], reads=(bh2,), writes=(bpT2,))
            S.op('V', lambda e: e.tensor_copy(
                h2fm[:, :, 0:128], pT2[:, 0:1024].rearrange("p (a b) -> p a b", a=8)),
                reads=(bpT2,), writes=(bh2fm,))
            S.op('V', lambda e: e.tensor_copy(
                h2fm[:, :, 128:256],
                pT2[:, 1024:2048].rearrange("p (a b) -> p a b", a=8)),
                reads=(bpT2,), writes=(bh2fm,))

            znb, bznb = zn[g % 2], bzn[g % 2]
            if debug and l == 0:
                S.op('SD', lambda e, g_=g: e.dma_start(
                    out=tapH2F_d.ap()[g_ * 128:(g_ + 1) * 128, :, :], in_=h2fm[:]),
                    reads=(bh2fm,))
            def mm_w1(e, c):
                w, off = _pair_geom(c)
                pag = pAG[c % 2]
                pt = partof[c]
                wsp = w1sp[pt]
                off = off - PLO[pt][0]
                for k in range(8):
                    e.matmul(pag[0:w, 0:256], wsp[:, k, off:off + w],
                             h2fm[:, k, :], start=(k == 0), stop=False)
                for k in range(8):
                    ins = e.matmul(pag[0:w, 256:512],
                                   wsp[:, k, off + w:off + 2 * w],
                                   h2fm[:, k, :], start=(k == 0), stop=(k == 7))
                return ins

            def mm_w2(e, c):
                w, _ = _pair_geom(c)
                wk = w if c < 21 else 43
                u_ = ub[c % 2] if c < 21 else u21
                for a in range(2):
                    pz = pZ[a]
                    for hf in range(2):
                        ins = e.matmul(pz[:, hf * DI:(hf + 1) * DI],
                                       u_[0:wk, a * TL:(a + 1) * TL],
                                       w2s[0:wk, c, hf * DI:(hf + 1) * DI],
                                       start=(c == 0), stop=(c == NPAIR - 1))
                return ins
            S.op('T', lambda e: mm_w1(e, 0), deps=[w1deps[0]],
                 reads=(bh2fm,), writes=(bpAG[0],))
            for c in range(NPAIR):
                w, off = _pair_geom(c)
                pag, bpag = pAG[c % 2], bpAG[c % 2]
                uu = ub[c % 2] if c < 21 else u21
                buu = bub[c % 2] if c < 21 else bu21
                S.op('S', lambda e, w_=w, c_=c, pg=pag: e.activation(
                    gg[0:w_, :], pg[0:w_, 256:512], AF.Gelu,
                    bias=b1_s[0:w_, 2 * c_ + 1:2 * c_ + 2]),
                    reads=(bpag, bB1), writes=(bgg,))
                if c + 1 < NPAIR:
                    S.op('T', lambda e, c_=c + 1: mm_w1(e, c_),
                         deps=[w1deps[partof[c + 1]]],
                         reads=(bh2fm,), writes=(bpAG[(c + 1) % 2],))
                S.op('V', lambda e, w_=w, c_=c, pg=pag, u_=uu: e.scalar_tensor_tensor(
                    out=u_[0:w_, :], in0=pg[0:w_, 0:256],
                    scalar=b1_s[0:w_, 2 * c_:2 * c_ + 1], in1=gg[0:w_, :],
                    op0=ALU.add, op1=ALU.mult),
                    reads=(bpag, bgg, bB1), writes=(buu,))
                if debug and l == 0 and c == 2:
                    S.op('SD', lambda e, g_=g, u_=uu: e.dma_start(
                        out=tapU_d.ap()[g_ * 128:(g_ + 1) * 128, :], in_=u_[0:128, :]),
                        reads=(buu,))
                S.op('T', lambda e, c_=c: mm_w2(e, c_),
                     deps=[w2deps[partof[c]]],
                     reads=(buu,), writes=(bpZ[0], bpZ[1]))

            for a in range(2):
                S.op('V', lambda e, a_=a: e.tensor_add(
                    znb[:, a_ * D:(a_ + 1) * D], pZ[a_][:],
                    zb[:, a_ * D:(a_ + 1) * D]),
                    reads=(bpZ[a], bzb), writes=(bznb,))
            sname2 = 'z2' if last else 'zs'
            S.op('SD', lambda e, g_=g: e.dma_start(
                out=dst_d.ap()[g_ * 256:(g_ + 1) * 256, :].rearrange(
                    "(a p) d -> p a d", p=128),
                in_=znb[:].rearrange("p (a d) -> p a d", a=2)), reads=(bznb,),
                writes=(B(f"{sname2}@{2 * g}"), B(f"{sname2}@{2 * g + 1}")))
            if not last:
                S.op('V', lambda e: e.tensor_copy(znb16[:], znb[:]),
                     reads=(bznb,), writes=(B(f"znb16{tag}"),))
                S.op('SD', lambda e, g_=g: e.dma_start(
                    out=zsh_d.ap()[g_ * 256:(g_ + 1) * 256, 0:DI].rearrange(
                        "(a p) d -> p a d", p=128),
                    in_=znb16[:].rearrange("p (a d) -> p a d", a=2)[:, :, 0:DI]),
                    reads=(B(f"znb16{tag}"),),
                    writes=(B(f"zsh@{2 * g}"), B(f"zsh@{2 * g + 1}")))
                S.op('SD', lambda e, g_=g: e.dma_start(
                    out=zsh_d.ap()[g_ * 256 + 1:(g_ + 1) * 256 + 1, DI:D].rearrange(
                        "(a p) d -> p a d", p=128),
                    in_=znb16[:].rearrange("p (a d) -> p a d", a=2)[:, :, DI:D]),
                    reads=(B(f"znb16{tag}"),),
                    writes=(B(f"zsh@{2 * g}"), B(f"zsh@{2 * g + 1}")))

        gload(0)
        gload(1)
        for g in range(NG):
            if g + 2 < NG:
                gload(g + 2)
            emit_group(g)
        es.close()

    def final_pass():
        S.pass_start()
        es = ExitStack()
        z2t = [sbuf(f"cz{i}", [128, D], F32, es) for i in range(3)]
        x1t = [sbuf(f"cx{i}", [128, D], F32, es) for i in range(3)]
        ot = [sbuf(f"co{i}", [128, D], F32, es) for i in range(2)]
        lt = sbuf("clt", [128, D], F32, es)
        bcz = [B(f"cz{i}") for i in range(3)]
        bcx = [B(f"cx{i}") for i in range(3)]
        bco = [B("co0"), B("co1")]
        bclt = B("clt")

        def cload(t):
            S.op('Y', lambda e, t_=t: e.dma_start(
                out=z2t[t_ % 3][:], in_=z2_d.ap()[t_ * TL:(t_ + 1) * TL, :]),
                deps=[B(f"z2@{t}").w], writes=(bcz[t % 3],))
            S.op('Y', lambda e, t_=t: e.dma_start(
                out=x1t[t_ % 3][:], in_=x1_d.ap()[t_ * TL:(t_ + 1) * TL, :]),
                deps=[B(f"x1@{t}").w], writes=(bcx[t % 3],))
        cload(0)
        cload(1)
        for t in range(NT):
            if t + 2 < NT:
                cload(t + 2)
            zt, bz = z2t[t % 3], bcz[t % 3]

            def lnsrc(e, what, z_=zt):
                if what == 'sum':
                    return e.reduce_sum(stLN[:, 0:1], z_[:], axis=AX.X)
                return e.activation(sq_s[:], z_[:], AF.Square,
                                    accum_out=stLN[:, 1:2])
            ln_stats(lnsrc, (bz,))
            S.op('V', lambda e, z_=zt: e.tensor_scalar(
                out=lt[:], in0=z_[:], scalar1=stLN[:, 4:5], scalar2=stLN[:, 5:6],
                op0=ALU.mult, op1=ALU.add), reads=(bz,), fence_reads=(bLN,),
                writes=(bclt,))
            S.op('V', lambda e, x_=x1t[t % 3], o_=ot[t % 2]: e.tensor_add(
                o_[:], lt[:], x_[:]),
                reads=(bclt, bcx[t % 3]), writes=(bco[t % 2],))
            S.op('SD', lambda e, t_=t, o_=ot[t % 2]: e.dma_start(
                out=out_d.ap()[t_ * TL:(t_ + 1) * TL, :], in_=o_[:]),
                reads=(bco[t % 2],))
        es.close()

    if 'A' in passes:
        attn_pass('A')
    for l in range(2):
        if l in passes or f'B1{l}' in passes:
            attn_pass('B1', l)
        if l in passes or f'F{l}' in passes:
            ffn_pass(l, last=(l == 1))
    if 'C' in passes:
        final_pass()

    emit_es = ExitStack()
    sems = {q: emit_es.enter_context(nc.semaphore(f"sem{q}")) for q in _STEP}
    block = emit_es.enter_context(nc.Block())
    with emit_es:
        def emit_engine(phys):
            def body(h):
                for waits, fn, q in S.thunks[phys]:
                    for (dq, val) in waits:
                        h.wait_ge(sems[dq], val)
                    ins = fn(h)
                    ins.then_inc(sems[q], _STEP[q])
                if phys == 'sync':
                    for q2 in ('SD', 'GD', 'Y'):
                        if S.count[q2] > 0:
                            h.wait_ge(sems[q2], S.count[q2])
            return body
        block.tensor(emit_engine('tensor'))
        block.vector(emit_engine('vector'))
        block.scalar(emit_engine('scalar'))
        block.gpsimd(emit_engine('gpsimd'))
        block.sync(emit_engine('sync'))
    top.close()
    return nc


def kernel(L0_attn_x, L0_attn_Wqkv, L0_attn_Wout, L0_ffn_wqkv, L0_ffn_q_scale,
           L0_ffn_k_scale, L0_ffn_wo, L0_ffn_ln_g, L0_ffn_ln_b, L0_ffn_w1,
           L0_ffn_b1, L0_ffn_w2, L0_ffn_b2):
    C = _host_consts()
    perm = C['perm']
    f32 = np.float32

    x = np.ascontiguousarray(np.asarray(L0_attn_x, dtype=f32))
    shared = {
        "WqkvO": np.ascontiguousarray(np.asarray(L0_attn_Wqkv, f32)),
        "WoutO": np.ascontiguousarray(np.asarray(L0_attn_Wout, f32)),
        "mCAU": C['mCAU'], "mSUPCAU": C['mSUPCAU'], "mUPICAU": C['mUPICAU'],
        "identb": C['identb'], "shiftm": C['shiftm'], "pick127": C['pick127'],
        "oc": C['oc'], "osn": C['os'], "ic": C['ic'], "isn": C['is_'],
        "cvec": np.tile(np.array([[EPS, -0.5, -1.0, D ** -0.5, 1.0 / D, 0.0]], np.float32), (128, 1)),
    }
    for l in range(2):
        shared[f"wqkv{l}"] = np.ascontiguousarray(np.asarray(L0_ffn_wqkv[l], f32))
        shared[f"wo{l}"] = np.ascontiguousarray(np.asarray(L0_ffn_wo[l], f32))
        w1p_full = np.asarray(L0_ffn_w1[l], f32)[:, perm]
        _plo = [(0, 1536), (1536, 2816), (2816, 4096), (4096, 5460)]
        for i, (lo, hi) in enumerate(_plo):
            shared[f"w1q{l}p{i}"] = np.ascontiguousarray(w1p_full[:, lo:hi])
        b1p = np.asarray(L0_ffn_b1[l], f32)[perm]
        b1s = np.zeros((128, 44), f32)
        for c in range(NPAIR):
            w, off = _pair_geom(c)
            b1s[0:w, 2 * c] = b1p[off:off + w]
            b1s[0:w, 2 * c + 1] = b1p[off + w:off + 2 * w]
        shared[f"b1p{l}"] = b1s
        w2 = np.asarray(L0_ffn_w2[l], f32)
        w2x = np.zeros((NPAIR, 128, D), f32)
        for c in range(NPAIR):
            w = 128 if c < 21 else 42
            w2x[c, 0:w, :] = w2[c * 128:c * 128 + w, :]
        w2x[21, 42, :] = np.asarray(L0_ffn_b2[l], f32)
        shared[f"w2x{l}"] = np.ascontiguousarray(w2x)
        shared[f"qsT{l}"] = np.ascontiguousarray(
            np.tile(np.asarray(L0_ffn_q_scale[l], f32), H)[None, :])
        shared[f"ksT{l}"] = np.ascontiguousarray(
            np.tile(np.asarray(L0_ffn_k_scale[l], f32), H)[None, :])
        shared[f"lng{l}"] = np.ascontiguousarray(np.asarray(L0_ffn_ln_g[l], f32)[None, :])
        shared[f"lnb{l}"] = np.ascontiguousarray(np.asarray(L0_ffn_ln_b[l], f32)[None, :])

    nc = build()
    in_maps = [dict(shared, x0=np.ascontiguousarray(x[b])) for b in range(NCORES)]
    res = run_bass_kernel_spmd(nc, in_maps, core_ids=list(range(NCORES)))
    out = np.stack([np.asarray(res.results[b]["out"]) for b in range(NCORES)], axis=0)
    return out.astype(np.float32)
